# version 03
# baseline (speedup 1.0000x reference)
import contextlib
import numpy as np
import ml_dtypes

import concourse.bass as bass
import concourse.mybir as mybir
from concourse.alu_op_type import AluOpType
from concourse.bass_utils import run_bass_kernel_spmd

F32 = mybir.dt.float32
BF16 = mybir.dt.bfloat16
AF = mybir.ActivationFunctionType
AX = mybir.AxisListType.X
OP = AluOpType

import os as _os
T, C, NCH, E, H, D, FF = 4096, 128, 32, 1024, 16, 64, 4096
NCH = int(_os.environ.get("K_NCH", NCH))
T = NCH * C
MAGIC = 1.5 * 2.0**23
NEG = -3.0e38


def build_program():
    nc = bass.Bass("TRN2", target_bir_lowering=False,
                   detect_race_conditions=False)

    dx = nc.dram_tensor("x", [T, E], F32, kind="ExternalInput").ap()
    dWq = nc.dram_tensor("Wq", [E, E], F32, kind="ExternalInput").ap()
    dWk = nc.dram_tensor("Wk", [E, E], F32, kind="ExternalInput").ap()
    dWv = nc.dram_tensor("Wv", [E, E], F32, kind="ExternalInput").ap()
    dWg = nc.dram_tensor("Wg", [E, E], F32, kind="ExternalInput").ap()
    dWo = nc.dram_tensor("Wo", [E, E], F32, kind="ExternalInput").ap()
    dWa = nc.dram_tensor("Wa", [E, H], F32, kind="ExternalInput").ap()
    dAlog = nc.dram_tensor("A_log", [1, H], F32, kind="ExternalInput").ap()
    dDtb = nc.dram_tensor("dt_bias", [1, H], F32, kind="ExternalInput").ap()
    dRms = nc.dram_tensor("rms_w", [D, 1], F32, kind="ExternalInput").ap()
    dW1t = nc.dram_tensor("w1t", [E, FF], F32, kind="ExternalInput").ap()
    dW2t = nc.dram_tensor("w2t", [FF, E], F32, kind="ExternalInput").ap()
    dB1 = nc.dram_tensor("b1", [1, FF], F32, kind="ExternalInput").ap()
    dB2 = nc.dram_tensor("b2", [1, E], F32, kind="ExternalInput").ap()
    dTri = nc.dram_tensor("tri", [128, 128], F32, kind="ExternalInput").ap()
    dMsk = nc.dram_tensor("maskneg", [128, 128], BF16, kind="ExternalInput").ap()
    dBlkh = nc.dram_tensor("blkh", [128, 128], BF16, kind="ExternalInput").ap()
    dBlkht = nc.dram_tensor("blkht", [16, 1024], BF16, kind="ExternalInput").ap()
    dIdb = nc.dram_tensor("identb", [128, 128], BF16, kind="ExternalInput").ap()
    dIdf = nc.dram_tensor("identf", [128, 128], F32, kind="ExternalInput").ap()
    dOnp = nc.dram_tensor("onespar", [128, 2], BF16, kind="ExternalInput").ap()
    dOne1 = nc.dram_tensor("ones1", [1, 128], F32, kind="ExternalInput").ap()
    dOnec = nc.dram_tensor("onescol", [128, 1], F32, kind="ExternalInput").ap()
    dy = nc.dram_tensor("y", [T, E], F32, kind="ExternalOutput").ap()
    sXq = nc.dram_tensor("scr_xq", [T, E], BF16).ap()
    sAtt = nc.dram_tensor("scr_att", [T, E], BF16).ap()
    sG = nc.dram_tensor("scr_g", [T, E], BF16).ap()
    sW1q = nc.dram_tensor("scr_w1q", [E, FF], BF16).ap()
    sW2q = nc.dram_tensor("scr_w2q", [FF, E], BF16).ap()

    es = contextlib.ExitStack()
    esA = contextlib.ExitStack()
    sb = lambda n, s, dt=F32: es.enter_context(nc.sbuf_tensor(n, s, dt))
    sbA = lambda n, s, dt=F32: esA.enter_context(nc.sbuf_tensor(n, s, dt))
    psA_ = lambda n, s, dt=F32: esA.enter_context(nc.psum_tensor(n, s, dt))

    ops = []
    cnt = {"t": 0, "v": 0, "s": 0, "g": 0, "y": 0}
    opidx = {"t": 0, "v": 0, "s": 0, "g": 0, "y": 0}
    pending = {}
    dma_keys = []
    forced = set()
    HAZ = 16

    def emit(ek, fn, waits=(), inc=0, sem_key=None):
        tok = None
        opidx[ek] += 1
        if inc:
            k = sem_key or ek
            cnt[k] += inc
            tok = (k, cnt[k], ek, opidx[ek])
        w = list(waits) + pending.pop(ek, [])
        ops.append((ek, fn, tuple(x_ for x_ in w if x_), tok, inc))
        return tok

    class R:
        _n = [0]
        __slots__ = ("w", "rs", "key")
        def __init__(self):
            self.w = None
            self.rs = []
            self.key = None
        def dma_key(self):
            if self.key is None:
                R._n[0] += 1
                self.key = "d%d" % R._n[0]
                cnt[self.key] = 0
                dma_keys.append(self.key)
            return self.key
        def dr(self):
            return [self.w] if self.w else []
        def dw(self):
            return ([self.w] if self.w else []) + self.rs

    def do(ek, fns, reads=(), writes=(), inc=1, inc_each=False):
        deps = []
        for r in reads:
            deps += r.dr()
        for r in writes:
            deps += r.dw()
        best = {}
        for t_ in deps:
            k2, v2 = t_[0], t_[1]
            same = len(t_) > 2 and t_[2] == ek and t_[0] == ek
            if k2 == ek and not same:
                continue
            if same:
                if opidx[ek] - t_[3] >= HAZ:
                    continue
                forced.add(t_)
            if best.get(k2, 0) < v2:
                best[k2] = v2
        waits = list(best.items())
        sem_key = None
        if inc == 16:
            assert len(writes) >= 1
            sem_key = writes[0].dma_key()
        tok = None
        for i, fn in enumerate(fns):
            last = (i == len(fns) - 1)
            tok = emit(ek, fn, waits if i == 0 else (),
                       inc if (inc_each or last) else 0, sem_key=sem_key)
            if tok is not None and (inc_each or inc == 16):
                forced.add(tok)
        for r in reads:
            r.rs.append(tok)
        for r in writes:
            r.w = tok
            r.rs = []
        return tok

    def one(ek, fn, reads=(), writes=(), inc=1):
        return do(ek, [fn], reads=reads, writes=writes, inc=inc)

    def barrier():
        toks = [(k, cnt[k]) for k in cnt if cnt[k] > 0 and k not in opidx]
        toks += [(k, cnt[k]) for k in opidx if cnt[k] > 0]
        for ek in ["t", "v", "s", "g", "y"]:
            pending[ek] = [t_ for t_ in toks if t_[0] != ek]

    def d_load(dst_ap, src_ap):
        return lambda e: e.dma_start(out=dst_ap, in_=src_ap)

    def ts_(out, in0, s1, s2, o1, o2=None):
        if o2 is None:
            return lambda e: e.tensor_scalar(out=out, in0=in0, scalar1=s1,
                                             scalar2=None, op0=o1)
        return lambda e: e.tensor_scalar(out=out, in0=in0, scalar1=s1,
                                         scalar2=s2, op0=o1, op1=o2)

    IDB = sb("IDB", [128, 128], BF16)
    B1B = sb("B1B", [1, FF], BF16); B2B = sb("B2B", [1, E], BF16)
    WO = sb("WO", [128, 8192], BF16)
    DUMP = sb("DUMP", [128, FF], BF16)
    SXI = sb("SXI", [128, NCH])
    NEAB = sb("NEAB", [128, 16])
    RAROW = sb("RAROW", [1, 128]); RAROWB = sb("RAROWB", [1, 128], BF16)
    RBROW = sb("RBROW", [1, 128]); RBROWB = sb("RBROWB", [1, 128], BF16)
    coln = ["xsum", "xss", "xmax", "xmin", "m", "msq", "var", "lnv", "rs",
            "t1", "t2", "mx", "rmx", "qsc", "nmq", "bq", "bxn", "vp", "rvp",
            "m2n", "l2", "r2", "u1", "u2", "swc1", "swc1i", "swc2", "swc2i",
            "a1", "ra1", "a2", "ra2", "mxh", "ssh", "m2h", "sqm", "r2h",
            "smx", "rm2", "qs2", "tt1", "tt2", "wtot"]
    COL = {n: sb("c_" + n, [128, 1]) for n in coln}
    acoln = ["xsum", "xss", "xmax", "xmin", "m", "msq", "var", "lnv", "rs",
             "t1", "t2", "mx", "rmx", "qsc", "nmq", "bq", "bxn", "vp", "rvp",
             "m2n", "l2", "r2", "u1", "u2"]
    ACOL = [{n: sb("ac%d_%s" % (p, n), [128, 1]) for n in acoln}
            for p in range(2)]
    N16 = sb("N16", [128, 16])
    C1E5 = sb("C1E5", [128, 1]); C1E6 = sb("C1E6", [128, 1])
    C1E8 = sb("C1E8", [128, 1])

    rb = {}
    def RR(name):
        rb[name] = R()
        return rb[name]
    for name in ["consts", "B1Bw", "B2Bw", "WOw", "DUMP", "SXI", "NEAB",
                 "RAROW", "RAROWB", "RBROW", "RBROWB", "N16",
                 "scrXQ", "scrATT", "scrG", "scrW1", "scrW2", "yst"]:
        RR(name)
    rcol = {n: R() for n in coln}
    racol = [{n: R() for n in acoln} for p in range(2)]
    cst = rb["consts"]

    TRI = sbA("TRI", [128, 128]); MSKN = sbA("MSKN", [128, 128], BF16)
    BLKH = sbA("BLKH", [128, 128], BF16); BLKHT = sbA("BLKHT", [16, 1024], BF16)
    IDF = sbA("IDF", [128, 128])
    ONP = sbA("ONP", [128, 2], BF16); ONE1 = sbA("ONE1", [1, 128])
    ONEC = sbA("ONEC", [128, 1])
    ALOG = sbA("ALOG", [1, 16]); DTB = sbA("DTB", [1, 16])
    RMSC = sbA("RMSC", [128, 1])
    WQ = sbA("WQ", [128, 8192], BF16); WK = sbA("WK", [128, 8192], BF16)
    WV = sbA("WV", [128, 8192], BF16); WG = sbA("WG", [128, 8192], BF16)
    WAB = sbA("WAB", [128, 128], BF16)
    WSTG = [sbA("WSTG0", [128, E])] * 2
    WST = sbA("WST", [128, FF // 2]); WQP = sbA("WQP", [128, FF // 2], BF16)
    WABS = sbA("WABS", [128, 32])
    XCH = [sbA(f"XCH{i}", [128, E]) for i in range(2)]
    XN32P = [sbA(f"XN32_{i}", [128, E]) for i in range(2)]
    TRND = sbA("TRND", [128, E])
    XNF = sbA("XNF", [128, E], BF16)
    XQB = [sbA(f"XQB{i}", [128, E], BF16) for i in range(2)]
    QF = sbA("QF", [128, E], BF16); QSQ = sbA("QSQ", [128, E], BF16)
    QTF = sbA("QTF", [128, E], BF16); QTH = sbA("QTH", [128, E], BF16)
    KB = sbA("KB", [128, E], BF16); KHI = sbA("KHI", [128, E], BF16)
    KSQ = sbA("KSQ", [128, E], BF16)
    KHF = sbA("KHF", [128, E], BF16); KHT = sbA("KHT", [128, E], BF16)
    VB = sbA("VB", [128, E], BF16)
    GB = [sbA(f"GB{i}", [128, E], BF16) for i in range(2)]
    ATB = [sbA(f"ATB{i}", [128, E], BF16) for i in range(2)]
    EUW = sbA("EUW", [128, E])
    EBF = [sbA(f"EBF{i}", [128, 128], BF16) for i in range(2)]
    PBF = [sbA(f"PBF{i}", [128, 128], BF16) for i in range(2)]
    MSB = sbA("MSB", [64, E]); MS128 = sbA("MS128", [128, E], BF16)
    MTMP = sbA("MTMP", [64, E])
    LA = sbA("LA", [128, 16]); CUMT = sbA("CUMT", [128, 16])
    CUMF = sbA("CUMF", [16, 128])
    WNT = sbA("WNT", [128, 16]); WNF = sbA("WNF", [16, 128])
    UFM = sbA("UFM", [16, 128]); EUS = sbA("EUS", [16, 128])
    EWS = sbA("EWS", [16, 128])
    EUHI = sbA("EUHI", [16, 128], BF16); EULO = sbA("EULO", [16, 128], BF16)
    EWHI = sbA("EWHI", [16, 128], BF16); EWLO = sbA("EWLO", [16, 128], BF16)
    T16A = sbA("T16A", [16, 128]); T16B = sbA("T16B", [16, 128])
    UFLAT = sbA("UFLAT", [1, 2048])
    EGROW = sbA("EGROW", [1, 16]); GROW = sbA("GROW", [1, 16])
    EGC = sbA("EGC", [64, 16])
    SATT = sbA("SATT", [128, 16]); NSA = sbA("NSA", [128, 16])
    for name in ["WQw", "WKw", "WVw", "WGw", "WABw", "WSTG0", "WSTG1", "WST",
                 "WQP", "WABS", "XCH0", "XCH1", "XN32_0", "XN32_1", "XNF",
                 "TRND", "XQB0", "XQB1", "DUMPa", "DUMPb", "DUMPc",
                 "QF", "QSQ", "QTF", "QTH", "KB", "KHI", "KSQ", "KHF", "KHT",
                 "VB", "GB0", "GB1", "ATB0", "ATB1", "EUW", "EBF0", "EBF1",
                 "PBF0", "PBF1", "MSB", "MS128", "MTMP", "LA", "CUMT", "CUMF",
                 "WNT", "WNF", "UFM", "EUS", "EWS", "EUHI", "EULO", "EWHI",
                 "EWLO", "T16A", "T16B", "UFLAT", "EGROW", "GROW", "EGC",
                 "SATT", "NSA"]:
        RR(name)

    PQS = psA_("PQS", [128, 1024])
    PK = psA_("PK", [128, 1024])
    PVG = psA_("PVG", [128, 1024])
    PSM = psA_("PSM", [128, 512])
    PSM2 = psA_("PSM2", [128, 512])
    rPQS, rPK, rPVG, rPSM, rPSM2 = R(), R(), R(), R(), R()

    for dst, src in [(TRI[:], dTri), (MSKN[:], dMsk), (BLKH[:], dBlkh),
                     (BLKHT[:], dBlkht), (IDB[:], dIdb), (IDF[:], dIdf),
                     (ONP[:], dOnp), (ONE1[:], dOne1), (ONEC[:], dOnec),
                     (ALOG[:], dAlog), (DTB[:], dDtb),
                     (RMSC[0:64, :], dRms), (RMSC[64:128, :], dRms)]:
        one("y", d_load(dst, src), writes=[cst], inc=16)
    one("g", d_load(B1B[:], dB1), writes=[rb["B1Bw"]], inc=16)
    one("g", d_load(B2B[:], dB2), writes=[rb["B2Bw"]], inc=16)

    rstg = [rb["WSTG0"], rb["WSTG0"]]

    def w_cast(dw_ap, Wsb, rW, scale=None):
        for i in range(8):
            p = i % 2
            one("y", d_load(WSTG[p][:], dw_ap[i * 128:(i + 1) * 128, :]),
                reads=[rstg[p]], writes=[rstg[p]], inc=16)
            if scale is None:
                one("v", (lambda p=p, i=i, Wsb=Wsb: lambda e: e.tensor_copy(
                    out=Wsb[:, i * E:(i + 1) * E], in_=WSTG[p][:]))(),
                    reads=[rstg[p]], writes=[rW])
            else:
                one("s", (lambda p=p, i=i, Wsb=Wsb, scale=scale:
                          lambda e: e.activation(
                              out=Wsb[:, i * E:(i + 1) * E], in_=WSTG[p][:],
                              func=AF.Copy, scale=scale))(),
                    reads=[rstg[p], cst, rcol["tt1"]], writes=[rW])

    w_cast(dWk, WK, rb["WKw"])
    w_cast(dWq, WQ, rb["WQw"])
    w_cast(dWv, WV, rb["WVw"])
    w_cast(dWg, WG, rb["WGw"])
    for i in range(8):
        p = i % 2
        one("y", d_load(WSTG[p][:, 0:16], dWa[i * 128:(i + 1) * 128, :]),
            reads=[rstg[p]], writes=[rstg[p]], inc=16)
        one("v", (lambda p=p, i=i: lambda e: e.tensor_copy(
            out=WAB[:, i * 16:(i + 1) * 16], in_=WSTG[p][:, 0:16]))(),
            reads=[rstg[p]], writes=[rb["WABw"]])
    one("v", ts_(COL["tt1"][:], RMSC[:], 0.5, None, OP.mult),
        reads=[cst], writes=[rcol["tt1"]])
    w_cast(dWo, WO, rb["WOw"], scale=COL["tt1"][:])

    one("s", lambda e: e.activation(out=EGROW[:], in_=ALOG[:], func=AF.Exp),
        reads=[cst], writes=[rb["EGROW"]])
    one("v", ts_(EGROW[:], EGROW[:], -1.0, None, OP.mult),
        reads=[rb["EGROW"]], writes=[rb["EGROW"]])
    one("t", lambda e: e.matmul(PSM[:, 0:16], ONE1[:], EGROW[:],
                                start=True, stop=True),
        reads=[cst, rb["EGROW"], rPSM], writes=[rPSM])
    one("v", lambda e: e.tensor_copy(out=NEAB[:], in_=PSM[:, 0:16]),
        reads=[rPSM], writes=[rb["NEAB"]])
    for buf, rn in [(MS128, "MS128"), (MSB, "MSB"), (QTH, "QTH"), (KHI, "KHI")]:
        one("v", (lambda b=buf: lambda e: e.memset(b[:], 0.0))(),
            writes=[rb[rn]])
    one("v", lambda e: e.memset(C1E5[:], 1e-5), writes=[cst])
    one("v", lambda e: e.memset(C1E6[:], 1e-6), writes=[cst])
    one("v", lambda e: e.memset(C1E8[:], 1e-8), writes=[cst])

    rwst = rb["WST"]

    def wpiece(src, idx):
        ncols = src.shape[1]
        per_row = max(1, ncols // 2048)
        cw = min(ncols, 2048)
        i, hh = idx // per_row, idx % per_row
        return src[i * 128:(i + 1) * 128, hh * cw:(hh + 1) * cw], cw

    def wq_abs(idx, src, colidx):
        sl, cw = wpiece(src, idx)
        one("y", d_load(WST[:, 0:cw], sl), reads=[rwst], writes=[rwst], inc=16)
        one("s", (lambda ci=colidx, cw=cw: lambda e: e.activation(
            out=DUMP[:, 2048:2048 + cw], in_=WST[:, 0:cw], func=AF.Abs,
            accum_out=WABS[:, ci:ci + 1]))(),
            reads=[rwst, rb["DUMPc"]], writes=[rb["WABS"], rb["DUMPc"]])

    def wq_mean(swc, swci, nelem, npieces):
        one("v", (lambda npieces=npieces: lambda e: e.tensor_reduce(
            out=COL["wtot"][:], in_=WABS[:, 0:npieces], axis=AX, op=OP.add))(),
            reads=[rb["WABS"]], writes=[rcol["wtot"]])
        one("t", lambda e: e.matmul(PSM[0:1, 496:497], COL["wtot"][:], ONEC[:],
                                    start=True, stop=True),
            reads=[rcol["wtot"], cst, rPSM], writes=[rPSM])
        one("v", ts_(COL["tt2"][0:1, :], PSM[0:1, 496:497], 1.0 / nelem, 1e-5,
                     OP.mult, OP.max),
            reads=[rPSM], writes=[rcol["tt2"]])
        one("t", lambda e: e.matmul(PSM[:, 497:498], ONE1[:], COL["tt2"][0:1, :],
                                    start=True, stop=True),
            reads=[rcol["tt2"], cst, rPSM], writes=[rPSM])
        one("v", (lambda swci=swci: lambda e: e.tensor_copy(
            out=COL[swci][:], in_=PSM[:, 497:498]))(),
            reads=[rPSM], writes=[rcol[swci]])
        one("v", (lambda swc=swc, swci=swci: lambda e: e.reciprocal(
            out=COL[swc][:], in_=COL[swci][:]))(),
            reads=[rcol[swci]], writes=[rcol[swc]])

    def wq_quant(idx, src, dst_scr, rdst, swc):
        sl, cw = wpiece(src, idx)
        dsl, _ = wpiece(dst_scr, idx)
        one("y", d_load(WST[:, 0:cw], sl), reads=[rwst], writes=[rwst], inc=16)
        one("s", (lambda swc=swc, cw=cw: lambda e: e.activation(
            out=WST[:, 0:cw], in_=WST[:, 0:cw], func=AF.Copy,
            scale=COL[swc][:], bias=MAGIC))(),
            reads=[rwst, rcol[swc]], writes=[rwst])
        one("v", (lambda cw=cw: ts_(WST[:, 0:cw], WST[:, 0:cw], -MAGIC, 1.0,
                                    OP.add, OP.min))(),
            reads=[rwst], writes=[rwst])
        one("v", (lambda cw=cw: ts_(WQP[:, 0:cw], WST[:, 0:cw], -1.0, None,
                                    OP.max))(),
            reads=[rwst], writes=[rb["WQP"]])
        one("g", (lambda dsl=dsl, cw=cw: d_load(dsl, WQP[:, 0:cw]))(),
            reads=[rb["WQP"]], writes=[rdst], inc=16)

    ACUT = float(_os.environ.get("K_ACUT", "99"))

    def phaseA_front(c):
        if c >= NCH:
            return
        p = c % 2
        xch, xq = XCH[p], XQB[p]
        rx, rxq = rb[f"XCH{p}"], rb[f"XQB{p}"]
        XN32 = XN32P[p]
        cl = lambda n: ACOL[p][n][:]
        rc = lambda n: racol[p][n]

        one("y", d_load(xch[:], dx[c * C:(c + 1) * C, :]),
            reads=[rx], writes=[rx], inc=16)

        if ACUT < 1:
            return
        one("v", lambda e: e.reduce_sum(out=cl("xsum"), in_=xch[:], axis=AX),
            reads=[rx], writes=[rc("xsum")])
        one("s", lambda e: e.activation(out=DUMP[:, E:2 * E], in_=xch[:],
                                        func=AF.Square, accum_out=cl("xss")),
            reads=[rx, rb["DUMPb"]], writes=[rc("xss"), rb["DUMPb"]])
        one("v", lambda e: e.reduce_max(out=cl("xmax"), in_=xch[:], axis=AX),
            reads=[rx], writes=[rc("xmax")])
        one("v", lambda e: e.tensor_reduce(out=cl("xmin"), in_=xch[:],
                                           axis=AX, op=OP.min),
            reads=[rx], writes=[rc("xmin")])
        one("v", ts_(cl("m"), cl("xsum"), 1.0 / E, None, OP.mult),
            reads=[rc("xsum")], writes=[rc("m")])
        one("v", lambda e: e.tensor_mul(cl("msq"), cl("m"), cl("m")),
            reads=[rc("m")], writes=[rc("msq")])
        one("v", lambda e: e.scalar_tensor_tensor(
            out=cl("var"), in0=cl("xss"), scalar=1.0 / E, in1=cl("msq"),
            op0=OP.mult, op1=OP.subtract),
            reads=[rc("xss"), rc("msq")], writes=[rc("var")])
        one("s", lambda e: e.activation(out=cl("lnv"), in_=cl("var"),
                                        func=AF.Ln, bias=C1E5[:]),
            reads=[rc("var")], writes=[rc("lnv")])
        one("s", lambda e: e.activation(out=cl("rs"), in_=cl("lnv"),
                                        func=AF.Exp, scale=-0.5),
            reads=[rc("lnv")], writes=[rc("rs")])
        one("v", lambda e: e.tensor_sub(cl("t1"), cl("xmax"), cl("m")),
            reads=[rc("xmax"), rc("m")], writes=[rc("t1")])
        one("v", lambda e: e.tensor_sub(cl("t2"), cl("m"), cl("xmin")),
            reads=[rc("m"), rc("xmin")], writes=[rc("t2")])
        one("v", lambda e: e.tensor_max(cl("mx"), cl("t1"), cl("t2")),
            reads=[rc("t1"), rc("t2")], writes=[rc("mx")])
        one("v", lambda e: e.reciprocal(out=cl("rmx"), in_=cl("mx")),
            reads=[rc("mx")], writes=[rc("rmx")])
        one("v", ts_(cl("qsc"), cl("rmx"), 127.0, None, OP.mult),
            reads=[rc("rmx")], writes=[rc("qsc")])
        one("v", lambda e: e.tensor_mul(cl("nmq"), cl("m"), cl("qsc")),
            reads=[rc("m"), rc("qsc")], writes=[rc("nmq")])
        one("v", ts_(cl("bq"), cl("nmq"), -1.0, MAGIC, OP.mult, OP.add),
            reads=[rc("nmq")], writes=[rc("bq")])
        one("v", lambda e: e.scalar_tensor_tensor(
            out=cl("bxn"), in0=cl("m"), scalar=-1.0, in1=cl("rs"),
            op0=OP.mult, op1=OP.mult),
            reads=[rc("m"), rc("rs")], writes=[rc("bxn")])
        one("v", ts_(cl("vp"), cl("var"), 1e-5, None, OP.add),
            reads=[rc("var")], writes=[rc("vp")])
        one("v", lambda e: e.reciprocal(out=cl("rvp"), in_=cl("vp")),
            reads=[rc("vp")], writes=[rc("rvp")])
        one("v", lambda e: e.tensor_mul(cl("m2n"), cl("var"), cl("rvp")),
            reads=[rc("var"), rc("rvp")], writes=[rc("m2n")])
        one("s", lambda e: e.activation(out=cl("l2"), in_=cl("m2n"),
                                        func=AF.Ln, bias=C1E6[:]),
            reads=[rc("m2n")], writes=[rc("l2")])
        one("s", lambda e: e.activation(out=cl("r2"), in_=cl("l2"),
                                        func=AF.Exp, scale=-0.5),
            reads=[rc("l2")], writes=[rc("r2")])
        one("v", lambda e: e.tensor_mul(cl("u1"), cl("r2"), cl("rs")),
            reads=[rc("r2"), rc("rs")], writes=[rc("u1")])
        one("v", lambda e: e.tensor_mul(cl("u2"), cl("u1"), cl("mx")),
            reads=[rc("u1"), rc("mx")], writes=[rc("u2")])
        one("v", ts_(SXI[:, c:c + 1], cl("u2"), 1e-5, 1.0 / 127.0,
                     OP.max, OP.mult),
            reads=[rc("u2")], writes=[rb["SXI"]])

        one("v", lambda e: e.scalar_tensor_tensor(
            out=XN32[:], in0=xch[:], scalar=cl("rs"),
            in1=cl("bxn").broadcast_to([128, E]),
            op0=OP.mult, op1=OP.add),
            reads=[rx, rc("rs"), rc("bxn")], writes=[rb[f"XN32_{p}"]])
        if ACUT < 2:
            return
        one("v", lambda e: e.scalar_tensor_tensor(
            out=TRND[:], in0=xch[:], scalar=cl("qsc"),
            in1=cl("bq").broadcast_to([128, E]),
            op0=OP.mult, op1=OP.add),
            reads=[rx, rc("qsc"), rc("bq"), rb["TRND"]], writes=[rb["TRND"]])
        one("v", (lambda xq=xq: lambda e: e.tensor_scalar(
            out=xq[:], in0=TRND[:], scalar1=-MAGIC, scalar2=None, op0=OP.add))(),
            reads=[rb["TRND"]], writes=[rxq])
        one("g", d_load(sXq[c * C:(c + 1) * C, :], xq[:]),
            reads=[rxq], writes=[rb["scrXQ"]], inc=16)

    def phaseA_chunk(c):
        p = c % 2
        xch = XCH[p]
        rx = rb[f"XCH{p}"]
        gb, atb = GB[p], ATB[p]
        rgb, ratb = rb[f"GB{p}"], rb[f"ATB{p}"]
        XN32 = XN32P[p]
        rxn = rb[f"XN32_{p}"]

        do("t", [(lambda i=i: lambda e: e.transpose(
            PQS[:, i * 128:(i + 1) * 128], XN32[:, i * 128:(i + 1) * 128],
            IDF[:]))() for i in range(8)],
            reads=[rxn, cst, rPQS], writes=[rPQS])
        one("v", lambda e: e.tensor_copy(out=XNF[:], in_=PQS[:]),
            reads=[rPQS], writes=[rb["XNF"]])

        if ACUT < 3:
            return
        fns = []
        for m in range(8):
            for kt in range(8):
                fns.append((lambda m=m, kt=kt: lambda e: e.matmul(
                    PK[:, m * 128:(m + 1) * 128],
                    WK[:, kt * E + m * 128: kt * E + (m + 1) * 128],
                    XNF[:, kt * 128:(kt + 1) * 128],
                    start=(kt == 0), stop=(kt == 7)))())
        do("t", fns, reads=[rb["WKw"], rb["XNF"], rPK], writes=[rPK])
        if ACUT < 3.2:
            return
        one("v", lambda e: e.tensor_copy(out=KB[:], in_=PK[:]),
            reads=[rPK], writes=[rb["KB"]])
        one("v", lambda e: e.tensor_copy(out=KHI[64:128, :], in_=PK[64:128, :]),
            reads=[rPK], writes=[rb["KHI"]])
        one("s", lambda e: e.activation(out=KSQ[:], in_=KB[:], func=AF.Square),
            reads=[rb["KB"]], writes=[rb["KSQ"]])
        if ACUT < 3.4:
            return
        fns = []
        for m in range(8):
            for kt in range(8):
                fns.append((lambda m=m, kt=kt: lambda e: e.matmul(
                    PQS[:, m * 128:(m + 1) * 128],
                    WQ[:, kt * E + m * 128: kt * E + (m + 1) * 128],
                    XNF[:, kt * 128:(kt + 1) * 128],
                    start=(kt == 0), stop=(kt == 7)))())
        do("t", fns, reads=[rb["WQw"], rb["XNF"], rPQS], writes=[rPQS])
        one("v", lambda e: e.tensor_copy(out=QF[:], in_=PQS[:]),
            reads=[rPQS], writes=[rb["QF"]])
        one("s", lambda e: e.activation(out=QSQ[:], in_=QF[:], func=AF.Square),
            reads=[rb["QF"]], writes=[rb["QSQ"]])
        if ACUT < 3.6:
            return
        fns = []
        for ns in range(2):
            for kt in range(8):
                fns.append((lambda ns=ns, kt=kt: lambda e: e.matmul(
                    PVG[:, ns * 512:(ns + 1) * 512],
                    XNF[:, kt * 128:(kt + 1) * 128],
                    WV[:, kt * E + ns * 512: kt * E + (ns + 1) * 512],
                    start=(kt == 0), stop=(kt == 7)))())
        do("t", fns, reads=[rb["WVw"], rb["XNF"], rPVG], writes=[rPVG])
        one("v", lambda e: e.tensor_copy(out=VB[:], in_=PVG[:]),
            reads=[rPVG], writes=[rb["VB"]])
        if ACUT < 3.8:
            return
        fns = []
        for ns in range(2):
            for kt in range(8):
                fns.append((lambda ns=ns, kt=kt: lambda e: e.matmul(
                    PVG[:, ns * 512:(ns + 1) * 512],
                    XNF[:, kt * 128:(kt + 1) * 128],
                    WG[:, kt * E + ns * 512: kt * E + (ns + 1) * 512],
                    start=(kt == 0), stop=(kt == 7)))())
        do("t", fns, reads=[rb["WGw"], rb["XNF"], rPVG], writes=[rPVG])
        one("v", (lambda gb=gb: lambda e: e.tensor_copy(out=gb[:], in_=PVG[:]))(),
            reads=[rPVG], writes=[rgb])
        one("g", d_load(sG[c * C:(c + 1) * C, :], gb[:]),
            reads=[rgb], writes=[rb["scrG"]], inc=16)

        phaseA_front(c + 1)

        if ACUT < 4:
            return
        fns = [(lambda kt=kt: lambda e: e.matmul(
            PSM[:, 32:48], XNF[:, kt * 128:(kt + 1) * 128],
            WAB[:, kt * 16:(kt + 1) * 16], start=(kt == 0), stop=False))()
            for kt in range(8)]
        fns.append(lambda e: e.matmul(PSM[:, 32:48], ONE1[:], DTB[:],
                                      start=False, stop=True))
        do("t", fns, reads=[rb["WABw"], rb["XNF"], cst, rPSM], writes=[rPSM])
        one("s", lambda e: e.activation(out=N16[:], in_=PSM[:, 32:48],
                                        func=AF.Exp),
            reads=[rPSM], writes=[rb["N16"]])
        one("s", lambda e: e.activation(out=N16[:], in_=N16[:], func=AF.Ln,
                                        bias=1.0),
            reads=[rb["N16"]], writes=[rb["N16"]])
        one("v", lambda e: e.tensor_mul(N16[:], N16[:], NEAB[:]),
            reads=[rb["N16"], rb["NEAB"]], writes=[rb["N16"]])
        one("s", lambda e: e.activation(out=N16[:], in_=N16[:], func=AF.Exp),
            reads=[rb["N16"]], writes=[rb["N16"]])
        one("s", lambda e: e.activation(out=LA[:], in_=N16[:], func=AF.Ln,
                                        bias=C1E8[:]),
            reads=[rb["N16"]], writes=[rb["LA"]])
        one("t", lambda e: e.matmul(PSM[:, 48:64], TRI[:], LA[:],
                                    start=True, stop=True),
            reads=[cst, rb["LA"], rPSM], writes=[rPSM])
        one("v", lambda e: e.tensor_copy(out=CUMT[:], in_=PSM[:, 48:64]),
            reads=[rPSM], writes=[rb["CUMT"]])
        one("t", lambda e: e.matmul(PSM[0:16, 64:192], LA[:], TRI[:],
                                    start=True, stop=True),
            reads=[cst, rb["LA"], rPSM], writes=[rPSM])
        one("v", lambda e: e.tensor_copy(out=CUMF[:], in_=PSM[0:16, 64:192]),
            reads=[rPSM], writes=[rb["CUMF"]])
        one("g", d_load(GROW[:], CUMT[127:128, :]),
            reads=[rb["CUMT"]], writes=[rb["GROW"]], inc=16)

        if ACUT < 5:
            return
        fns = [(lambda f=f: lambda e: e.matmul(
            PSM[0:16, 192:320], BLKH[:, f * 16:(f + 1) * 16],
            QSQ[:, f * 128:(f + 1) * 128], start=(f == 0), stop=(f == 7)))()
            for f in range(8)]
        do("t", fns, reads=[cst, rb["QSQ"], rPSM], writes=[rPSM])
        one("v", ts_(T16A[:], PSM[0:16, 192:320], 1e-24, None, OP.max),
            reads=[rPSM], writes=[rb["T16A"]])
        one("s", lambda e: e.activation(out=T16A[:], in_=T16A[:], func=AF.Ln),
            reads=[rb["T16A"]], writes=[rb["T16A"]])
        one("v", lambda e: e.scalar_tensor_tensor(
            out=UFM[:], in0=T16A[:], scalar=-0.5, in1=CUMF[:],
            op0=OP.mult, op1=OP.add),
            reads=[rb["T16A"], rb["CUMF"]], writes=[rb["UFM"]])
        one("s", lambda e: e.activation(out=EUS[:], in_=UFM[:], func=AF.Exp),
            reads=[rb["UFM"]], writes=[rb["EUS"]])
        one("v", lambda e: e.tensor_copy(out=EUHI[:], in_=EUS[:]),
            reads=[rb["EUS"]], writes=[rb["EUHI"]])
        one("v", lambda e: e.tensor_copy(out=T16B[:], in_=EUHI[:]),
            reads=[rb["EUHI"]], writes=[rb["T16B"]])
        one("v", lambda e: e.tensor_sub(EULO[:], EUS[:], T16B[:]),
            reads=[rb["EUS"], rb["T16B"]], writes=[rb["EULO"]])
        one("g", d_load(UFLAT[:], UFM[:]),
            reads=[rb["UFM"]], writes=[rb["UFLAT"]], inc=16)

        fns = [(lambda h=h: lambda e: e.matmul(
            PSM[:, 320 + h:321 + h],
            KSQ[:, (h // 2) * 128:(h // 2 + 1) * 128],
            ONP[:, (h % 2):(h % 2) + 1], start=True, stop=True))()
            for h in range(16)]
        do("t", fns, reads=[cst, rb["KSQ"], rPSM], writes=[rPSM])
        one("v", ts_(N16[:], PSM[:, 320:336], 1e-24, None, OP.max),
            reads=[rPSM, rb["N16"]], writes=[rb["N16"]])
        one("s", lambda e: e.activation(out=N16[:], in_=N16[:], func=AF.Ln),
            reads=[rb["N16"]], writes=[rb["N16"]])
        one("v", lambda e: e.scalar_tensor_tensor(
            out=WNT[:], in0=N16[:], scalar=-0.5, in1=CUMT[:],
            op0=OP.mult, op1=OP.subtract),
            reads=[rb["N16"], rb["CUMT"]], writes=[rb["WNT"]])

        fns = [(lambda f=f: lambda e: e.matmul(
            PSM[0:16, 336:464], BLKH[:, f * 16:(f + 1) * 16],
            KSQ[:, f * 128:(f + 1) * 128], start=(f == 0), stop=(f == 7)))()
            for f in range(8)]
        do("t", fns, reads=[cst, rb["KSQ"], rPSM], writes=[rPSM])
        one("v", ts_(T16B[:], PSM[0:16, 336:464], 1e-24, None, OP.max),
            reads=[rPSM, rb["T16B"]], writes=[rb["T16B"]])
        one("s", lambda e: e.activation(out=T16B[:], in_=T16B[:], func=AF.Ln),
            reads=[rb["T16B"]], writes=[rb["T16B"]])
        one("v", lambda e: e.scalar_tensor_tensor(
            out=WNF[:], in0=T16B[:], scalar=-0.5, in1=CUMF[:],
            op0=OP.mult, op1=OP.subtract),
            reads=[rb["T16B"], rb["CUMF"]], writes=[rb["WNF"]])
        one("s", lambda e: e.activation(out=EWS[:], in_=WNF[:], func=AF.Exp,
                                        bias=CUMF[:, 127:128]),
            reads=[rb["WNF"], rb["CUMF"]], writes=[rb["EWS"]])
        one("v", lambda e: e.tensor_copy(out=EWHI[:], in_=EWS[:]),
            reads=[rb["EWS"]], writes=[rb["EWHI"]])
        one("v", lambda e: e.tensor_copy(out=T16B[:], in_=EWHI[:]),
            reads=[rb["EWHI"], rb["T16B"]], writes=[rb["T16B"]])
        one("v", lambda e: e.tensor_sub(EWLO[:], EWS[:], T16B[:]),
            reads=[rb["EWS"], rb["T16B"]], writes=[rb["EWLO"]])

        one("s", lambda e: e.activation(out=EGROW[:], in_=GROW[:], func=AF.Exp),
            reads=[rb["GROW"], rb["EGROW"]], writes=[rb["EGROW"]])
        one("t", lambda e: e.matmul(PSM[0:64, 464:480], ONE1[:, 0:64],
                                    EGROW[:], start=True, stop=True),
            reads=[cst, rb["EGROW"], rPSM], writes=[rPSM])
        one("v", lambda e: e.tensor_copy(out=EGC[:], in_=PSM[0:64, 464:480]),
            reads=[rPSM], writes=[rb["EGC"]])

        if ACUT < 6:
            return
        fns = []
        for f in range(8):
            for j, src in ((0, EUHI), (1, EULO)):
                fns.append((lambda f=f, j=j, src=src: lambda e: e.matmul(
                    PVG[:, f * 128:(f + 1) * 128],
                    BLKHT[:, f * 128:(f + 1) * 128], src[:],
                    start=(j == 0), stop=(j == 1)))())
        do("t", fns, reads=[cst, rb["EUHI"], rb["EULO"], rPVG], writes=[rPVG])
        one("v", lambda e: e.tensor_copy(out=EUW[:], in_=PVG[:]),
            reads=[rPVG], writes=[rb["EUW"]])
        one("v", lambda e: e.tensor_mul(QTF[:], PQS[:], EUW[:]),
            reads=[rPQS, rb["EUW"]], writes=[rb["QTF"]])
        one("v", lambda e: e.tensor_mul(QTH[64:128, :], PQS[64:128, :],
                                        EUW[64:128, :]),
            reads=[rPQS, rb["EUW"]], writes=[rb["QTH"]])
        fns = []
        for f in range(8):
            for j, src in ((0, EWHI), (1, EWLO)):
                fns.append((lambda f=f, j=j, src=src: lambda e: e.matmul(
                    PVG[:, f * 128:(f + 1) * 128],
                    BLKHT[:, f * 128:(f + 1) * 128], src[:],
                    start=(j == 0), stop=(j == 1)))())
        do("t", fns, reads=[cst, rb["EWHI"], rb["EWLO"], rPVG], writes=[rPVG])
        one("v", lambda e: e.tensor_copy(out=EUW[:], in_=PVG[:]),
            reads=[rPVG, rb["EUW"]], writes=[rb["EUW"]])
        one("v", lambda e: e.tensor_mul(KHF[:], PK[:], EUW[:]),
            reads=[rPK, rb["EUW"]], writes=[rb["KHF"]])
        do("t", [(lambda i=i: lambda e: e.transpose(
            PSM2.bitcast(BF16)[:, i * 128:(i + 1) * 128],
            KHF[:, i * 128:(i + 1) * 128],
            IDB[:]))() for i in range(8)],
            reads=[rb["KHF"], cst, rPSM2], writes=[rPSM2])
        one("v", lambda e: e.tensor_copy(out=KHT[:], in_=PSM2.bitcast(BF16)[:]),
            reads=[rPSM2], writes=[rb["KHT"]])

        if ACUT < 7:
            return
        for r in range(2):
            fns = []
            for s_ in range(8):
                h = 8 * r + s_
                f = h // 2
                if h % 2 == 0:
                    fns.append((lambda s_=s_, f=f: lambda e: e.matmul(
                        PQS[:, s_ * 128:(s_ + 1) * 128],
                        KB[0:64, f * 128:(f + 1) * 128],
                        QF[0:64, f * 128:(f + 1) * 128],
                        start=True, stop=True))())
                else:
                    fns.append((lambda s_=s_, f=f: lambda e: e.matmul(
                        PQS[:, s_ * 128:(s_ + 1) * 128],
                        KHI[:, f * 128:(f + 1) * 128],
                        QF[:, f * 128:(f + 1) * 128],
                        start=True, stop=True))())
            do("t", fns, reads=[rb["KB"], rb["KHI"], rb["QF"], rPQS],
               writes=[rPQS])
            for s_ in range(8):
                h = 8 * r + s_
                f = h // 2
                pp = h % 2
                rU = rPSM if pp == 0 else rPSM2
                PU = PSM if pp == 0 else PSM2
                rE, rP = rb[f"EBF{pp}"], rb[f"PBF{pp}"]
                do("t", [
                    (lambda PU=PU, h=h: lambda e: e.matmul(
                        PU[:, 0:128], ONE1[:],
                        UFLAT[0:1, h * 128:(h + 1) * 128],
                        start=True, stop=False))(),
                    (lambda PU=PU: lambda e: e.matmul(
                        PU[:, 0:128], IDB[:], MSKN[:],
                        start=False, stop=True))()],
                    reads=[cst, rb["UFLAT"], rU], writes=[rU])
                one("s", (lambda PU=PU, pp=pp, h=h: lambda e: e.activation(
                    out=EBF[pp][:], in_=PU[:, 0:128], func=AF.Exp,
                    bias=WNT[:, h:h + 1]))(),
                    reads=[rU, rb["WNT"], rE], writes=[rE])
                one("v", (lambda pp=pp, s_=s_: lambda e: e.tensor_mul(
                    PBF[pp][:], PQS[:, s_ * 128:(s_ + 1) * 128],
                    EBF[pp][:]))(),
                    reads=[rPQS, rE, rP], writes=[rP])
                fns = [(lambda pp=pp, h=h: lambda e: e.matmul(
                    PVG[:, h * 64:(h + 1) * 64], PBF[pp][:],
                    VB[:, h * 64:(h + 1) * 64], start=True, stop=False))()]
                if h % 2 == 0:
                    fns.append((lambda h=h, f=f: lambda e: e.matmul(
                        PVG[:, h * 64:(h + 1) * 64],
                        QTF[0:64, f * 128:(f + 1) * 128],
                        MS128[0:64, h * 64:(h + 1) * 64],
                        start=False, stop=True))())
                else:
                    fns.append((lambda h=h, f=f: lambda e: e.matmul(
                        PVG[:, h * 64:(h + 1) * 64],
                        QTH[:, f * 128:(f + 1) * 128],
                        MS128[:, h * 64:(h + 1) * 64],
                        start=False, stop=True))())
                do("t", fns, reads=[rP, rb["VB"], rb["QTF"], rb["QTH"],
                                    rb["MS128"], rPVG], writes=[rPVG])
                one("t", (lambda h=h: lambda e: e.matmul(
                    PK[0:64, h * 64:(h + 1) * 64],
                    KHT[:, h * 64:(h + 1) * 64],
                    VB[:, h * 64:(h + 1) * 64], start=True, stop=True))(),
                    reads=[rb["KHT"], rb["VB"], rPK], writes=[rPK])

        if ACUT < 8:
            return
        one("v", lambda e: e.tensor_tensor(
            out=MTMP[:], in0=MSB[:],
            in1=EGC[:].unsqueeze(2).broadcast_to([64, 16, 64]),
            op=OP.mult),
            reads=[rb["MSB"], rb["EGC"], rb["MTMP"]], writes=[rb["MTMP"]])
        one("v", lambda e: e.tensor_add(MSB[:], PK[0:64, :], MTMP[:]),
            reads=[rPK, rb["MTMP"], rb["MSB"]], writes=[rb["MSB"]])
        one("v", lambda e: e.tensor_copy(out=MS128[0:64, :], in_=MSB[:]),
            reads=[rb["MSB"], rb["MS128"]], writes=[rb["MS128"]])
        one("g", d_load(MS128[64:128, :], MS128[0:64, :]),
            reads=[rb["MS128"]], writes=[rb["MS128"]], inc=16)

        if ACUT < 9:
            return
        one("v", lambda e: e.tensor_copy(out=XN32[:], in_=PVG[:]),
            reads=[rPVG, rxn], writes=[rxn])
        one("s", lambda e: e.activation(out=DUMP[:, 0:E], in_=XN32[:],
                                        func=AF.Square),
            reads=[rxn, rb["DUMPa"]], writes=[rb["DUMPa"]])
        one("v", lambda e: e.tensor_reduce(
            out=NSA[:], in_=DUMP[:, 0:E].rearrange("p (g k) -> p g k", k=64),
            axis=AX, op=OP.add),
            reads=[rb["DUMPa"]], writes=[rb["NSA"]])
        one("v", ts_(NSA[:], NSA[:], 1.0 / 64.0, 1e-6, OP.mult, OP.add),
            reads=[rb["NSA"]], writes=[rb["NSA"]])
        one("s", lambda e: e.activation(out=NSA[:], in_=NSA[:], func=AF.Ln),
            reads=[rb["NSA"]], writes=[rb["NSA"]])
        one("s", lambda e: e.activation(out=SATT[:], in_=NSA[:], func=AF.Exp,
                                        scale=-0.5),
            reads=[rb["NSA"]], writes=[rb["SATT"]])
        one("v", (lambda atb=atb: lambda e: e.tensor_tensor(
            out=atb[:], in0=XN32[:],
            in1=SATT[:].unsqueeze(2).broadcast_to([128, 16, 64]),
            op=OP.mult))(),
            reads=[rxn, rb["SATT"], ratb], writes=[ratb])
        one("g", d_load(sAtt[c * C:(c + 1) * C, :], atb[:]),
            reads=[ratb], writes=[rb["scrATT"]], inc=16)

    phaseA_front(0)
    if NCH < 32:
        for idx in range(16):
            wq_abs(idx, dW1t, idx)
        wq_mean("swc1", "swc1i", E * FF, 16)
        for idx in range(16):
            wq_quant(idx, dW1t, sW1q, rb["scrW1"], "swc1")
        for idx in range(32):
            wq_abs(idx, dW2t, idx)
        wq_mean("swc2", "swc2i", E * FF, 32)
        for idx in range(32):
            wq_quant(idx, dW2t, sW2q, rb["scrW2"], "swc2")
    for c in range(NCH):
        phaseA_chunk(c)
        if NCH < 32:
            continue
        if c < 8:
            wq_abs(2 * c, dW1t, 2 * c)
            wq_abs(2 * c + 1, dW1t, 2 * c + 1)
        elif c < 16:
            if c == 8:
                wq_mean("swc1", "swc1i", E * FF, 16)
            wq_quant(2 * (c - 8), dW1t, sW1q, rb["scrW1"], "swc1")
            wq_quant(2 * (c - 8) + 1, dW1t, sW1q, rb["scrW1"], "swc1")
        elif c < 24:
            for j in range(4):
                wq_abs(4 * (c - 16) + j, dW2t, 4 * (c - 16) + j)
        else:
            if c == 24:
                wq_mean("swc2", "swc2i", E * FF, 32)
            for j in range(4):
                wq_quant(4 * (c - 24) + j, dW2t, sW2q, rb["scrW2"], "swc2")

    if _os.environ.get("K_DBG"):
        for cn in ["xsum", "xss", "xmax", "xmin", "m", "var", "rs", "bxn",
                   "qsc", "bq", "mx", "msq", "t1", "t2", "rmx", "nmq"]:
            dob = nc.dram_tensor("dbgc_" + cn, [128, 1], F32,
                                 kind="ExternalOutput").ap()
            one("y", d_load(dob, COL[cn][:]), reads=[rcol[cn]],
                writes=[rb["yst"]], inc=16)
        dbgs = [("UFLAT", UFLAT, [1, 2048], F32), ("WNT", WNT, [128, 16], F32),
                ("LA", LA, [128, 16], F32), ("CUMT", CUMT, [128, 16], F32),
                ("CUMF", CUMF, [16, 128], F32), ("UFM", UFM, [16, 128], F32),
                ("KB", KB, [128, E], BF16), ("QF", QF, [128, E], BF16),
                ("ATB0", ATB[0], [128, E], BF16), ("EBF0", EBF[0], [128, 128], BF16),
                ("PBF0", PBF[0], [128, 128], BF16), ("VB", VB, [128, E], BF16),
                ("MS", MS128, [128, E], BF16), ("XNF", XNF, [128, E], BF16),
                ("KHT", KHT, [128, E], BF16), ("QTF", QTF, [128, E], BF16)]
        for nm, buf, shp, dt_ in dbgs:
            dob = nc.dram_tensor("dbg_" + nm, shp, dt_, kind="ExternalOutput").ap()
            one("y", d_load(dob, buf[:]), reads=[rb.get(nm) or R()],
                writes=[rb["yst"]], inc=16)

    barrier()
    esA.close()

    esB = contextlib.ExitStack()
    sbB = lambda n, s, dt=F32: esB.enter_context(nc.sbuf_tensor(n, s, dt))
    psB_ = lambda n, s, dt=F32: esB.enter_context(nc.psum_tensor(n, s, dt))

    W1Q = sbB("W1Q", [128, 8 * FF], BF16)
    W2Q = sbB("W2Q", [128, 32 * E], BF16)
    ATBB = [sbB("ATBB0", [128, E], BF16)] * 2
    GBB = [sbB("GBB0", [128, E], BF16)] * 2
    XRES = [sbB("XRES0", [128, E])] * 2
    XQF = sbB("XQF", [128, E], BF16)
    HG = sbB("HG", [128, FF])
    HQS = [sbB("HQS0", [128, 512], BF16)] * 2
    HQF = [sbB("HQF0", [128, E], BF16)] * 2
    GT = sbB("GT", [128, E])
    GT2 = sbB("GT2", [128, E], BF16)
    ATG = sbB("ATG", [128, E], BF16)
    ATGF = ATG
    YOUT = [GT] * 2
    for name in ["W1Qw", "W2Qw", "ATBB0", "GBB0", "XRES0",
                 "XQF", "HG", "HQS0", "HQF0", "GT",
                 "GT2", "ATG", "ATGF"]:
        RR(name)
    rb["ATBB1"] = rb["ATBB0"]; rb["GBB1"] = rb["GBB0"]
    rb["XRES1"] = rb["XRES0"]; rb["HQS1"] = rb["HQS0"]
    rb["HQF1"] = rb["HQF0"]; rb["ATGF"] = rb["ATG"]
    rb["YOUT0"] = rb["GT"]; rb["YOUT1"] = rb["GT"]

    PH1 = psB_("PH1", [128, 512]); PH2 = psB_("PH2", [128, 512])
    PY = psB_("PY", [128, 1024])
    PYW = psB_("PYW", [128, 1024])
    PB_TB1 = psB_("PB_TB1", [128, 1024], BF16)
    PB_TB2 = psB_("PB_TB2", [128, 1024], BF16)
    PB_TR = PB_TB1
    rPH = [R(), R()]
    rPY, rPYW, rPTB = R(), R(), [R(), R()]
    rPTR = rPTB[0]
    ONESB = sbB("ONESB", [1, 128], BF16)
    rb["ONESB"] = R()
    one("v", lambda e: e.memset(ONESB[:], 1.0), writes=[rb["ONESB"]])

    do("y", [(lambda i=i: d_load(W1Q[:, i * FF:(i + 1) * FF],
                                 sW1q[i * 128:(i + 1) * 128, :]))()
             for i in range(8)],
       reads=[rb["scrW1"]], writes=[rb["W1Qw"]], inc=16, inc_each=True)
    do("y", [(lambda i=i: d_load(W2Q[:, i * E:(i + 1) * E],
                                 sW2q[i * 128:(i + 1) * 128, :]))()
             for i in range(32)],
       reads=[rb["scrW2"]], writes=[rb["W2Qw"]], inc=16, inc_each=True)

    def phaseB_chunk(c):
        p = c % 2
        cl = lambda n: COL[n][:]
        rc = lambda n: rcol[n]
        atb, gbb, xres, yout = ATBB[p], GBB[p], XRES[p], YOUT[p]
        ratb, rgbb = rb[f"ATBB{p}"], rb[f"GBB{p}"]
        rxres, ryout = rb[f"XRES{p}"], rb[f"YOUT{p}"]

        one("y", d_load(atb[:], sAtt[c * C:(c + 1) * C, :]),
            reads=[rb["scrATT"], ratb], writes=[ratb], inc=16)
        one("y", d_load(gbb[:], sG[c * C:(c + 1) * C, :]),
            reads=[rb["scrG"], rgbb], writes=[rgbb], inc=16)
        one("y", d_load(xres[:], dx[c * C:(c + 1) * C, :]),
            reads=[rxres], writes=[rxres], inc=16)
        do("y", [(lambda i=i: lambda e: e.dma_start_transpose(
            out=XQF[:, i * 128:(i + 1) * 128],
            in_=sXq[c * C:(c + 1) * C, i * 128:(i + 1) * 128]))()
            for i in range(8)],
            reads=[rb["scrXQ"], rb["XQF"]], writes=[rb["XQF"]], inc=16,
           inc_each=True)

        one("v", lambda e: e.tensor_mul(cl("a1"), SXI[:, c:c + 1],
                                        COL["swc1i"][:]),
            reads=[rb["SXI"], rcol["swc1i"]], writes=[rc("a1")])
        one("v", lambda e: e.reciprocal(out=cl("ra1"), in_=cl("a1")),
            reads=[rc("a1")], writes=[rc("ra1")])
        one("g", d_load(RAROW[:], COL["ra1"][:]),
            reads=[rc("ra1"), rb["RAROW"]], writes=[rb["RAROW"]], inc=16)
        one("v", lambda e: e.tensor_copy(out=RAROWB[:], in_=RAROW[:]),
            reads=[rb["RAROW"], rb["RAROWB"]], writes=[rb["RAROWB"]])

        for os in range(8):
            pp = os % 2
            fns = [(lambda os=os, kt=kt, pp=pp: lambda e: e.matmul(
                (PH1 if pp == 0 else PH2)[:, 0:512],
                XQF[:, kt * 128:(kt + 1) * 128],
                W1Q[:, kt * FF + os * 512: kt * FF + (os + 1) * 512],
                start=(kt == 0), stop=False))() for kt in range(8)]
            fns.append((lambda os=os, pp=pp: lambda e: e.matmul(
                (PH1 if pp == 0 else PH2)[:, 0:512], RAROWB[:],
                B1B[0:1, os * 512:(os + 1) * 512],
                start=False, stop=True))())
            do("t", fns, reads=[rb["W1Qw"], rb["XQF"], rb["RAROWB"],
                                rb["B1Bw"], rPH[pp]], writes=[rPH[pp]])
            one("s", (lambda os=os, pp=pp: lambda e: e.activation(
                out=HG[:, os * 512:(os + 1) * 512],
                in_=(PH1 if pp == 0 else PH2)[:, 0:512],
                func=AF.Gelu, scale=cl("a1")))(),
                reads=[rPH[pp], rc("a1"), rb["HG"]], writes=[rb["HG"]])

        one("s", (lambda gbb=gbb: lambda e: e.activation(
            out=GT[:], in_=gbb[:], func=AF.Tanh, scale=0.5))(),
            reads=[rgbb, rb["GT"]], writes=[rb["GT"]])
        one("v", (lambda gbb=gbb: lambda e: e.scalar_tensor_tensor(
            out=GT2[:], in0=GT[:], scalar=1.0, in1=gbb[:],
            op0=OP.add, op1=OP.mult))(),
            reads=[rb["GT"], rgbb, rb["GT2"]], writes=[rb["GT2"]])
        one("v", (lambda atb=atb: lambda e: e.tensor_mul(ATG[:], atb[:],
                                                         GT2[:]))(),
            reads=[ratb, rb["GT2"], rb["ATG"]], writes=[rb["ATG"]])
        do("t", [(lambda i=i: lambda e: e.transpose(
            PB_TR[:, i * 128:(i + 1) * 128], ATG[:, i * 128:(i + 1) * 128],
            IDB[:]))() for i in range(8)],
            reads=[rb["ATG"], cst, rPTR], writes=[rPTR])
        one("v", lambda e: e.tensor_copy(out=ATGF[:], in_=PB_TR[:]),
            reads=[rPTR, rb["ATGF"]], writes=[rb["ATGF"]])
        fns = []
        for ns in range(2):
            fns.append((lambda ns=ns: lambda e: e.matmul(
                PYW[:, ns * 512:(ns + 1) * 512], ONESB[:],
                B2B[0:1, ns * 512:(ns + 1) * 512],
                start=True, stop=False))())
            for kt in range(8):
                fns.append((lambda ns=ns, kt=kt: lambda e: e.matmul(
                    PYW[:, ns * 512:(ns + 1) * 512],
                    ATGF[:, kt * 128:(kt + 1) * 128],
                    WO[:, kt * E + ns * 512: kt * E + (ns + 1) * 512],
                    start=False, stop=(kt == 7)))())
        do("t", fns, reads=[rb["ATGF"], rb["WOw"], rb["ONESB"], rb["B2Bw"],
                            rPYW], writes=[rPYW])

        one("v", lambda e: e.reduce_max(out=cl("mxh"), in_=HG[:], axis=AX,
                                        apply_absolute_value=True),
            reads=[rb["HG"]], writes=[rc("mxh")])
        one("s", lambda e: e.activation(out=DUMP[:], in_=HG[:],
                                        func=AF.Square, accum_out=cl("ssh")),
            reads=[rb["HG"], rb["DUMPa"], rb["DUMPb"], rb["DUMPc"]],
            writes=[rc("ssh"), rb["DUMPa"], rb["DUMPb"], rb["DUMPc"]])
        one("v", ts_(cl("m2h"), cl("ssh"), 1.0 / FF, 1e-6, OP.mult, OP.add),
            reads=[rc("ssh")], writes=[rc("m2h")])
        one("s", lambda e: e.activation(out=cl("sqm"), in_=cl("m2h"),
                                        func=AF.Sqrt),
            reads=[rc("m2h")], writes=[rc("sqm")])
        one("v", lambda e: e.reciprocal(out=cl("r2h"), in_=cl("sqm")),
            reads=[rc("sqm")], writes=[rc("r2h")])
        one("v", ts_(cl("smx"), cl("mxh"), 1e-20, None, OP.max),
            reads=[rc("mxh")], writes=[rc("smx")])
        one("v", lambda e: e.reciprocal(out=cl("rm2"), in_=cl("smx")),
            reads=[rc("smx")], writes=[rc("rm2")])
        one("v", ts_(cl("qs2"), cl("rm2"), 127.0, None, OP.mult),
            reads=[rc("rm2")], writes=[rc("qs2")])
        one("v", lambda e: e.tensor_mul(cl("tt1"), cl("r2h"), cl("mxh")),
            reads=[rc("r2h"), rc("mxh")], writes=[rc("tt1")])
        one("v", ts_(cl("tt2"), cl("tt1"), 1e-5, 1.0 / 127.0, OP.max, OP.mult),
            reads=[rc("tt1")], writes=[rc("tt2")])
        one("v", lambda e: e.tensor_mul(cl("a2"), cl("tt2"), COL["swc2i"][:]),
            reads=[rc("tt2"), rcol["swc2i"]], writes=[rc("a2")])


        one("s", lambda e: e.activation(out=HG[:], in_=HG[:], func=AF.Copy,
                                        scale=cl("qs2"), bias=MAGIC),
            reads=[rb["HG"], rc("qs2")], writes=[rb["HG"]])
        for q4 in range(4):
            pp = q4 % 2
            rHq, rHf = rb[f"HQS{pp}"], rb[f"HQF{pp}"]
            PT, rPT = (PB_TB1, rPTB[0]) if pp == 0 else (PB_TB2, rPTB[1])
            for half in range(2):
                one("v", (lambda q4=q4, half=half, pp=pp:
                          lambda e: e.tensor_scalar(
                              out=HQS[pp][:, 0:512],
                              in0=HG[:, q4 * 1024 + half * 512:
                                      q4 * 1024 + (half + 1) * 512],
                              scalar1=-MAGIC, scalar2=None, op0=OP.add))(),
                    reads=[rb["HG"], rHq], writes=[rHq])
                do("t", [(lambda i=i, half=half, pp=pp, PT=PT:
                          lambda e: e.transpose(
                              PT[:, (half * 4 + i) * 128:
                                 (half * 4 + i + 1) * 128],
                              HQS[pp][:, i * 128:(i + 1) * 128], IDB[:]))()
                         for i in range(4)],
                   reads=[rHq, cst, rPT], writes=[rPT])
            one("v", (lambda pp=pp, PT=PT: lambda e: e.tensor_copy(
                out=HQF[pp][:], in_=PT[:]))(),
                reads=[rPT, rHf], writes=[rHf])
            for ns in range(2):
                fns = []
                for i in range(8):
                    kt = q4 * 8 + i
                    fns.append((lambda i=i, kt=kt, ns=ns, pp=pp, q4=q4:
                                lambda e: e.matmul(
                                    PY[:, ns * 512:(ns + 1) * 512],
                                    HQF[pp][:, i * 128:(i + 1) * 128],
                                    W2Q[:, kt * E + ns * 512:
                                        kt * E + (ns + 1) * 512],
                                    start=(q4 == 0 and i == 0),
                                    stop=(q4 == 3 and i == 7)))())
                do("t", fns, reads=[rHf, rb["W2Qw"], rPY], writes=[rPY])
        one("v", (lambda yout=yout, xres=xres:
                  lambda e: e.scalar_tensor_tensor(
                      out=yout[:], in0=PY[:], scalar=cl("a2"), in1=xres[:],
                      op0=OP.mult, op1=OP.add))(),
            reads=[rPY, rc("a2"), rxres, ryout], writes=[ryout])
        one("v", (lambda yout=yout: lambda e: e.tensor_add(
            yout[:], yout[:], PYW[:]))(),
            reads=[rPYW, ryout], writes=[ryout])
        one("y", d_load(dy[c * C:(c + 1) * C, :], yout[:]),
            reads=[ryout], writes=[rb["yst"]], inc=16)

    if _os.environ.get("K_PHA"):
        for c in range(NCH):
            one("y", d_load(dy[c * C:(c + 1) * C, :], XRES[0][:]),
                reads=[rb["XRES0"]], writes=[rb["yst"]], inc=16)
    else:
        for c in range(NCH):
            phaseB_chunk(c)

    ystk = rb["yst"].key
    final_y = cnt.get(ystk, 0)
    used = set()
    for (_, _, waits, _, _) in ops:
        used.update((w[0], w[1]) for w in waits)
    if ystk is not None:
        used.add((ystk, final_y))
    used |= set((t_[0], t_[1]) for t_ in forced)
    keyincs = {}
    for (_, _, _, tok, amt) in ops:
        if tok is not None:
            keyincs.setdefault(tok[0], []).append((tok[1], amt))
    for k in keyincs:
        keyincs[k].sort()
    remap = {}
    for k, incs in keyincs.items():
        newv = 0
        for (v, amt) in incs:
            if (k, v) in used:
                newv += amt
                remap[(k, v)] = newv
            else:
                remap[(k, v)] = None
    ops2 = []
    for (ek2, fn, waits, tok, amt) in ops:
        w2 = tuple((w[0], remap[(w[0], w[1])]) for w in waits)
        t2 = None
        if tok is not None and remap.get((tok[0], tok[1])) is not None:
            t2 = (tok[0], remap[(tok[0], tok[1])])
        ops2.append((ek2, fn, w2, t2, amt))
    final_y2 = remap.get((ystk, final_y)) if ystk is not None else None

    sems = {k: es.enter_context(nc.semaphore("sem_" + k))
            for k in ["t", "v", "s", "g", "y"] + dma_keys}

    def engine_body(ek):
        def body(e):
            wm = {}
            for (g2, fn, waits, tok, amt) in ops2:
                if g2 != ek:
                    continue
                for (we, wv) in waits:
                    if wm.get(we, 0) >= wv:
                        continue
                    e.wait_ge(sems[we], wv)
                    wm[we] = wv
                inst = fn(e)
                if tok is not None:
                    inst.then_inc(sems[tok[0]], amt)
            if ek == "y" and final_y2:
                e.wait_ge(sems[ystk], final_y2)
        return body

    engmap = {"t": "tensor", "v": "vector", "s": "scalar", "g": "gpsimd",
              "y": "sync"}
    with es, esB, nc.Block() as block:
        for ek in ["t", "v", "s", "g", "y"]:
            getattr(block, engmap[ek])(engine_body(ek))
    return nc


_NC_CACHE = None


def kernel(L0_attn_x, L0_attn_Wq, L0_attn_Wk, L0_attn_Wv, L0_attn_Wa,
           L0_attn_Wg, L0_attn_Wo, L0_attn_A_log, L0_attn_dt_bias,
           L0_attn_rms_w, L0_ffn_w1, L0_ffn_b1, L0_ffn_w2, L0_ffn_b2):
    global _NC_CACHE
    x = np.asarray(L0_attn_x, dtype=np.float32)
    B = x.shape[0]
    f32 = lambda a: np.ascontiguousarray(np.asarray(a, dtype=np.float32))
    bf16c = lambda a: np.ascontiguousarray(
        np.asarray(a, dtype=np.float32).astype(ml_dtypes.bfloat16))

    jj, ii = np.meshgrid(np.arange(128), np.arange(128), indexing="ij")
    tri = (jj <= ii).astype(np.float32)
    maskneg = np.where(jj <= ii, 0.0, NEG).astype(np.float32)
    blkh = np.zeros((128, 128), np.float32)
    for f in range(8):
        blkh[0:64, 16 * f + 2 * f] = 1.0
        blkh[64:128, 16 * f + 2 * f + 1] = 1.0
    blkht = np.zeros((16, 1024), np.float32)
    for f in range(8):
        blkht[2 * f, 128 * f:128 * f + 64] = 1.0
        blkht[2 * f + 1, 128 * f + 64:128 * (f + 1)] = 1.0
    onespar = np.zeros((128, 2), np.float32)
    onespar[0:64, 0] = 1.0
    onespar[64:128, 1] = 1.0

    base = {
        "Wq": f32(L0_attn_Wq), "Wk": f32(L0_attn_Wk), "Wv": f32(L0_attn_Wv),
        "Wg": f32(L0_attn_Wg), "Wo": f32(L0_attn_Wo), "Wa": f32(L0_attn_Wa),
        "A_log": f32(L0_attn_A_log).reshape(1, H),
        "dt_bias": f32(L0_attn_dt_bias).reshape(1, H),
        "rms_w": f32(L0_attn_rms_w).reshape(D, 1),
        "w1t": f32(np.asarray(L0_ffn_w1).T),
        "w2t": f32(np.asarray(L0_ffn_w2).T),
        "b1": f32(L0_ffn_b1).reshape(1, FF),
        "b2": f32(L0_ffn_b2).reshape(1, E),
        "tri": tri, "maskneg": bf16c(maskneg), "blkh": bf16c(blkh),
        "blkht": bf16c(blkht),
        "identb": np.eye(128, dtype=ml_dtypes.bfloat16),
        "identf": np.eye(128, dtype=np.float32),
        "onespar": bf16c(onespar),
        "ones1": np.ones((1, 128), np.float32),
        "onescol": np.ones((128, 1), np.float32),
    }
    in_maps = []
    for i in range(B):
        m = dict(base)
        m["x"] = np.ascontiguousarray(x[i])
        in_maps.append(m)

    if _NC_CACHE is None:
        _NC_CACHE = build_program()
    res = run_bass_kernel_spmd(_NC_CACHE, in_maps, core_ids=list(range(B)))
    return np.stack([np.asarray(r["y"], dtype=np.float32)
                     for r in res.results])


# version 04
# speedup vs baseline: 1.0794x; 1.0794x over previous
import contextlib
import numpy as np
import ml_dtypes

import concourse.bass as bass
import concourse.mybir as mybir
from concourse.alu_op_type import AluOpType
from concourse.bass_utils import run_bass_kernel_spmd

F32 = mybir.dt.float32
BF16 = mybir.dt.bfloat16
AF = mybir.ActivationFunctionType
AX = mybir.AxisListType.X
OP = AluOpType

import os as _os
T, C, NCH, E, H, D, FF = 4096, 128, 32, 1024, 16, 64, 4096
NCH = int(_os.environ.get("K_NCH", NCH))
T = NCH * C
MAGIC = 1.5 * 2.0**23
NEG = -3.0e38


def build_program():
    nc = bass.Bass("TRN2", target_bir_lowering=False,
                   detect_race_conditions=False)

    dx = nc.dram_tensor("x", [T, E], F32, kind="ExternalInput").ap()
    dWq = nc.dram_tensor("Wq", [E, E], F32, kind="ExternalInput").ap()
    dWk = nc.dram_tensor("Wk", [E, E], F32, kind="ExternalInput").ap()
    dWv = nc.dram_tensor("Wv", [E, E], F32, kind="ExternalInput").ap()
    dWg = nc.dram_tensor("Wg", [E, E], F32, kind="ExternalInput").ap()
    dWo = nc.dram_tensor("Wo", [E, E], F32, kind="ExternalInput").ap()
    dWa = nc.dram_tensor("Wa", [E, H], F32, kind="ExternalInput").ap()
    dAlog = nc.dram_tensor("A_log", [1, H], F32, kind="ExternalInput").ap()
    dDtb = nc.dram_tensor("dt_bias", [1, H], F32, kind="ExternalInput").ap()
    dRms = nc.dram_tensor("rms_w", [D, 1], F32, kind="ExternalInput").ap()
    dW1t = nc.dram_tensor("w1t", [E, FF], F32, kind="ExternalInput").ap()
    dW2t = nc.dram_tensor("w2t", [FF, E], F32, kind="ExternalInput").ap()
    dB1 = nc.dram_tensor("b1", [1, FF], F32, kind="ExternalInput").ap()
    dB2 = nc.dram_tensor("b2", [1, E], F32, kind="ExternalInput").ap()
    dTri = nc.dram_tensor("tri", [128, 128], F32, kind="ExternalInput").ap()
    dMsk = nc.dram_tensor("maskneg", [128, 128], BF16, kind="ExternalInput").ap()
    dBlkh = nc.dram_tensor("blkh", [128, 128], BF16, kind="ExternalInput").ap()
    dBlkht = nc.dram_tensor("blkht", [16, 1024], BF16, kind="ExternalInput").ap()
    dIdb = nc.dram_tensor("identb", [128, 128], BF16, kind="ExternalInput").ap()
    dIdf = nc.dram_tensor("identf", [128, 128], F32, kind="ExternalInput").ap()
    dOnp = nc.dram_tensor("onespar", [128, 2], BF16, kind="ExternalInput").ap()
    dOne1 = nc.dram_tensor("ones1", [1, 128], F32, kind="ExternalInput").ap()
    dOnec = nc.dram_tensor("onescol", [128, 1], F32, kind="ExternalInput").ap()
    dy = nc.dram_tensor("y", [T, E], F32, kind="ExternalOutput").ap()
    sXq = nc.dram_tensor("scr_xq", [T, E], BF16).ap()
    sAtt = nc.dram_tensor("scr_att", [T, E], BF16).ap()
    sG = nc.dram_tensor("scr_g", [T, E], BF16).ap()
    sW1q = nc.dram_tensor("scr_w1q", [E, FF], BF16).ap()
    sW2q = nc.dram_tensor("scr_w2q", [FF, E], BF16).ap()

    es = contextlib.ExitStack()
    esA = contextlib.ExitStack()
    sb = lambda n, s, dt=F32: es.enter_context(nc.sbuf_tensor(n, s, dt))
    sbA = lambda n, s, dt=F32: esA.enter_context(nc.sbuf_tensor(n, s, dt))
    psA_ = lambda n, s, dt=F32: esA.enter_context(nc.psum_tensor(n, s, dt))

    ops = []
    cnt = {"t": 0, "v": 0, "s": 0, "g": 0, "y": 0}
    opidx = {"t": 0, "v": 0, "s": 0, "g": 0, "y": 0}
    pending = {}
    dma_keys = []
    forced = set()
    HAZ = 16

    def emit(ek, fn, waits=(), inc=0, sem_key=None):
        tok = None
        opidx[ek] += 1
        if inc:
            k = sem_key or ek
            cnt[k] += inc
            tok = (k, cnt[k], ek, opidx[ek])
        w = list(waits) + pending.pop(ek, [])
        ops.append((ek, fn, tuple(x_ for x_ in w if x_), tok, inc))
        return tok

    class R:
        _n = [0]
        __slots__ = ("w", "rs", "key")
        def __init__(self):
            self.w = None
            self.rs = []
            self.key = None
        def dma_key(self):
            if self.key is None:
                R._n[0] += 1
                self.key = "d%d" % R._n[0]
                cnt[self.key] = 0
                dma_keys.append(self.key)
            return self.key
        def dr(self):
            return [self.w] if self.w else []
        def dw(self):
            return ([self.w] if self.w else []) + self.rs

    def do(ek, fns, reads=(), writes=(), inc=1, inc_each=False):
        deps = []
        for r in reads:
            deps += r.dr()
        for r in writes:
            deps += r.dw()
        best = {}
        for t_ in deps:
            k2, v2 = t_[0], t_[1]
            same = len(t_) > 2 and t_[2] == ek and t_[0] == ek
            if k2 == ek and not same:
                continue
            if same:
                if opidx[ek] - t_[3] >= HAZ:
                    continue
                forced.add(t_)
            if best.get(k2, 0) < v2:
                best[k2] = v2
        waits = list(best.items())
        sem_key = None
        if inc == 16:
            assert len(writes) >= 1
            sem_key = writes[0].dma_key()
        tok = None
        for i, fn in enumerate(fns):
            last = (i == len(fns) - 1)
            tok = emit(ek, fn, waits if i == 0 else (),
                       inc if (inc_each or last) else 0, sem_key=sem_key)
            if tok is not None and (inc_each or inc == 16):
                forced.add(tok)
        for r in reads:
            r.rs.append(tok)
        for r in writes:
            r.w = tok
            r.rs = []
        return tok

    def one(ek, fn, reads=(), writes=(), inc=1):
        return do(ek, [fn], reads=reads, writes=writes, inc=inc)

    def barrier():
        toks = [(k, cnt[k]) for k in cnt if cnt[k] > 0 and k not in opidx]
        toks += [(k, cnt[k]) for k in opidx if cnt[k] > 0]
        for ek in ["t", "v", "s", "g", "y"]:
            pending[ek] = [t_ for t_ in toks if t_[0] != ek]

    def d_load(dst_ap, src_ap):
        return lambda e: e.dma_start(out=dst_ap, in_=src_ap)

    def ts_(out, in0, s1, s2, o1, o2=None):
        if o2 is None:
            return lambda e: e.tensor_scalar(out=out, in0=in0, scalar1=s1,
                                             scalar2=None, op0=o1)
        return lambda e: e.tensor_scalar(out=out, in0=in0, scalar1=s1,
                                         scalar2=s2, op0=o1, op1=o2)

    IDB = sb("IDB", [128, 128], BF16)
    B1B = sb("B1B", [1, FF], BF16); B2B = sb("B2B", [1, E], BF16)
    WO = sb("WO", [128, 8192], BF16)
    DUMP = sb("DUMP", [128, FF], BF16)
    SXI = sb("SXI", [128, NCH])
    NEAB = sb("NEAB", [128, 16])
    RAROW = sb("RAROW", [1, 128]); RAROWB = sb("RAROWB", [1, 128], BF16)
    RBROW = sb("RBROW", [1, 128]); RBROWB = sb("RBROWB", [1, 128], BF16)
    coln = ["xsum", "xss", "xmax", "xmin", "m", "msq", "var", "lnv", "rs",
            "t1", "t2", "mx", "rmx", "qsc", "nmq", "bq", "bxn", "vp", "rvp",
            "m2n", "l2", "r2", "u1", "u2", "swc1", "swc1i", "swc2", "swc2i",
            "a1", "ra1", "a2", "ra2", "mxh", "ssh", "m2h", "sqm", "r2h",
            "smx", "rm2", "qs2", "tt1", "tt2", "wtot"]
    COL = {n: sb("c_" + n, [128, 1]) for n in coln}
    acoln = ["xsum", "xss", "xmax", "xmin", "m", "msq", "var", "lnv", "rs",
             "t1", "t2", "mx", "rmx", "qsc", "nmq", "bq", "bxn", "vp", "rvp",
             "m2n", "l2", "r2", "u1", "u2"]
    ACOL = [{n: sb("ac%d_%s" % (p, n), [128, 1]) for n in acoln}
            for p in range(2)]
    N16 = sb("N16", [128, 16])
    C1E5 = sb("C1E5", [128, 1]); C1E6 = sb("C1E6", [128, 1])
    C1E8 = sb("C1E8", [128, 1])

    rb = {}
    def RR(name):
        rb[name] = R()
        return rb[name]
    for name in ["consts", "B1Bw", "B2Bw", "WOw", "DUMP", "SXI", "NEAB",
                 "RAROW", "RAROWB", "RBROW", "RBROWB", "N16",
                 "scrXQ", "scrATT", "scrG", "scrW1", "scrW2", "yst"]:
        RR(name)
    rcol = {n: R() for n in coln}
    racol = [{n: R() for n in acoln} for p in range(2)]
    cst = rb["consts"]

    TRI = sbA("TRI", [128, 128]); MSKN = sbA("MSKN", [128, 128], BF16)
    BLKH = sbA("BLKH", [128, 128], BF16); BLKHT = sbA("BLKHT", [16, 1024], BF16)
    IDF = sbA("IDF", [128, 128])
    ONP = sbA("ONP", [128, 2], BF16); ONE1 = sbA("ONE1", [1, 128])
    ONEC = sbA("ONEC", [128, 1])
    ALOG = sbA("ALOG", [1, 16]); DTB = sbA("DTB", [1, 16])
    RMSC = sbA("RMSC", [128, 1])
    WQ = sbA("WQ", [128, 8192], BF16); WK = sbA("WK", [128, 8192], BF16)
    WV = sbA("WV", [128, 8192], BF16); WG = sbA("WG", [128, 8192], BF16)
    WAB = sbA("WAB", [128, 128], BF16)
    WSTG = [sbA("WSTG0", [128, E])] * 2
    WST = sbA("WST", [128, FF // 2]); WQP = sbA("WQP", [128, FF // 2], BF16)
    WABS = sbA("WABS", [128, 32])
    XCH = [sbA(f"XCH{i}", [128, E]) for i in range(2)]
    XN32P = [sbA(f"XN32_{i}", [128, E]) for i in range(2)]
    TRND = sbA("TRND", [128, E])
    XNF = sbA("XNF", [128, E], BF16)
    XQB = [sbA(f"XQB{i}", [128, E], BF16) for i in range(2)]
    QF = sbA("QF", [128, E], BF16); QSQ = sbA("QSQ", [128, E], BF16)
    QTF = sbA("QTF", [128, E], BF16); QTH = sbA("QTH", [128, E], BF16)
    KB = sbA("KB", [128, E], BF16); KHI = sbA("KHI", [128, E], BF16)
    KSQ = sbA("KSQ", [128, E], BF16)
    KHF = sbA("KHF", [128, E], BF16); KHT = sbA("KHT", [128, E], BF16)
    VB = sbA("VB", [128, E], BF16)
    GB = [sbA(f"GB{i}", [128, E], BF16) for i in range(2)]
    ATB = [sbA(f"ATB{i}", [128, E], BF16) for i in range(2)]
    EUW = sbA("EUW", [128, E])
    EBF = [sbA(f"EBF{i}", [128, 128], BF16) for i in range(4)]
    PBF = [sbA(f"PBF{i}", [128, 128], BF16) for i in range(4)]
    MSB = sbA("MSB", [64, E]); MS128 = sbA("MS128", [128, E], BF16)
    MTMP = sbA("MTMP", [64, E])
    LA = sbA("LA", [128, 16]); CUMT = sbA("CUMT", [128, 16])
    CUMF = sbA("CUMF", [16, 128])
    WNT = sbA("WNT", [128, 16]); WNF = sbA("WNF", [16, 128])
    UFM = sbA("UFM", [16, 128]); EUS = sbA("EUS", [16, 128])
    EWS = sbA("EWS", [16, 128])
    EUHI = sbA("EUHI", [16, 128], BF16); EULO = sbA("EULO", [16, 128], BF16)
    EWHI = sbA("EWHI", [16, 128], BF16); EWLO = sbA("EWLO", [16, 128], BF16)
    T16A = sbA("T16A", [16, 128]); T16B = sbA("T16B", [16, 128])
    UFLAT = sbA("UFLAT", [1, 2048])
    EGROW = sbA("EGROW", [1, 16]); GROW = sbA("GROW", [1, 16])
    EGC = sbA("EGC", [64, 16])
    SATT = sbA("SATT", [128, 16]); NSA = sbA("NSA", [128, 16])
    for name in ["WQw", "WKw", "WVw", "WGw", "WABw", "WSTG0", "WSTG1", "WST",
                 "WQP", "WABS", "XCH0", "XCH1", "XN32_0", "XN32_1", "XNF",
                 "TRND", "XQB0", "XQB1", "DUMPa", "DUMPb", "DUMPc",
                 "QF", "QSQ", "QTF", "QTH", "KB", "KHI", "KSQ", "KHF", "KHT",
                 "VB", "GB0", "GB1", "ATB0", "ATB1", "EUW", "EBF0", "EBF1",
                 "EBF2", "EBF3", "PBF0", "PBF1", "PBF2", "PBF3",
                 "MSB", "MS128", "MTMP", "LA", "CUMT", "CUMF",
                 "WNT", "WNF", "UFM", "EUS", "EWS", "EUHI", "EULO", "EWHI",
                 "EWLO", "T16A", "T16B", "UFLAT", "EGROW", "GROW", "EGC",
                 "SATT", "NSA"]:
        RR(name)

    PQS = psA_("PQS", [128, 1024])
    PK = psA_("PK", [128, 1024])
    PVG = psA_("PVG", [128, 1024])
    PSM = psA_("PSM", [128, 512])
    PSM2 = psA_("PSM2", [128, 512])
    rPQS, rPK, rPVG, rPSM, rPSM2 = R(), R(), R(), R(), R()

    for dst, src in [(TRI[:], dTri), (MSKN[:], dMsk), (BLKH[:], dBlkh),
                     (BLKHT[:], dBlkht), (IDB[:], dIdb), (IDF[:], dIdf),
                     (ONP[:], dOnp), (ONE1[:], dOne1), (ONEC[:], dOnec),
                     (ALOG[:], dAlog), (DTB[:], dDtb),
                     (RMSC[0:64, :], dRms), (RMSC[64:128, :], dRms)]:
        one("y", d_load(dst, src), writes=[cst], inc=16)
    one("g", d_load(B1B[:], dB1), writes=[rb["B1Bw"]], inc=16)
    one("g", d_load(B2B[:], dB2), writes=[rb["B2Bw"]], inc=16)

    rstg = [rb["WSTG0"], rb["WSTG0"]]

    def w_cast(dw_ap, Wsb, rW, scale=None):
        for i in range(8):
            p = i % 2
            one("y", d_load(WSTG[p][:], dw_ap[i * 128:(i + 1) * 128, :]),
                reads=[rstg[p]], writes=[rstg[p]], inc=16)
            if scale is None:
                one("v", (lambda p=p, i=i, Wsb=Wsb: lambda e: e.tensor_copy(
                    out=Wsb[:, i * E:(i + 1) * E], in_=WSTG[p][:]))(),
                    reads=[rstg[p]], writes=[rW])
            else:
                one("s", (lambda p=p, i=i, Wsb=Wsb, scale=scale:
                          lambda e: e.activation(
                              out=Wsb[:, i * E:(i + 1) * E], in_=WSTG[p][:],
                              func=AF.Copy, scale=scale))(),
                    reads=[rstg[p], cst, rcol["tt1"]], writes=[rW])

    w_cast(dWk, WK, rb["WKw"])
    w_cast(dWq, WQ, rb["WQw"])
    w_cast(dWv, WV, rb["WVw"])
    w_cast(dWg, WG, rb["WGw"])
    for i in range(8):
        p = i % 2
        one("y", d_load(WSTG[p][:, 0:16], dWa[i * 128:(i + 1) * 128, :]),
            reads=[rstg[p]], writes=[rstg[p]], inc=16)
        one("v", (lambda p=p, i=i: lambda e: e.tensor_copy(
            out=WAB[:, i * 16:(i + 1) * 16], in_=WSTG[p][:, 0:16]))(),
            reads=[rstg[p]], writes=[rb["WABw"]])
    one("v", ts_(COL["tt1"][:], RMSC[:], 0.5, None, OP.mult),
        reads=[cst], writes=[rcol["tt1"]])
    w_cast(dWo, WO, rb["WOw"], scale=COL["tt1"][:])

    one("s", lambda e: e.activation(out=EGROW[:], in_=ALOG[:], func=AF.Exp),
        reads=[cst], writes=[rb["EGROW"]])
    one("v", ts_(EGROW[:], EGROW[:], -1.0, None, OP.mult),
        reads=[rb["EGROW"]], writes=[rb["EGROW"]])
    one("t", lambda e: e.matmul(PSM[:, 0:16], ONE1[:], EGROW[:],
                                start=True, stop=True),
        reads=[cst, rb["EGROW"], rPSM], writes=[rPSM])
    one("v", lambda e: e.tensor_copy(out=NEAB[:], in_=PSM[:, 0:16]),
        reads=[rPSM], writes=[rb["NEAB"]])
    for buf, rn in [(MS128, "MS128"), (MSB, "MSB"), (QTH, "QTH"), (KHI, "KHI")]:
        one("v", (lambda b=buf: lambda e: e.memset(b[:], 0.0))(),
            writes=[rb[rn]])
    one("v", lambda e: e.memset(C1E5[:], 1e-5), writes=[cst])
    one("v", lambda e: e.memset(C1E6[:], 1e-6), writes=[cst])
    one("v", lambda e: e.memset(C1E8[:], 1e-8), writes=[cst])

    rwst = rb["WST"]

    def wpiece(src, idx):
        ncols = src.shape[1]
        per_row = max(1, ncols // 2048)
        cw = min(ncols, 2048)
        i, hh = idx // per_row, idx % per_row
        return src[i * 128:(i + 1) * 128, hh * cw:(hh + 1) * cw], cw

    def wq_abs(idx, src, colidx):
        sl, cw = wpiece(src, idx)
        one("y", d_load(WST[:, 0:cw], sl), reads=[rwst], writes=[rwst], inc=16)
        one("s", (lambda ci=colidx, cw=cw: lambda e: e.activation(
            out=DUMP[:, 2048:2048 + cw], in_=WST[:, 0:cw], func=AF.Abs,
            accum_out=WABS[:, ci:ci + 1]))(),
            reads=[rwst, rb["DUMPc"]], writes=[rb["WABS"], rb["DUMPc"]])

    def wq_mean(swc, swci, nelem, npieces):
        one("v", (lambda npieces=npieces: lambda e: e.tensor_reduce(
            out=COL["wtot"][:], in_=WABS[:, 0:npieces], axis=AX, op=OP.add))(),
            reads=[rb["WABS"]], writes=[rcol["wtot"]])
        one("t", lambda e: e.matmul(PSM[0:1, 496:497], COL["wtot"][:], ONEC[:],
                                    start=True, stop=True),
            reads=[rcol["wtot"], cst, rPSM], writes=[rPSM])
        one("v", ts_(COL["tt2"][0:1, :], PSM[0:1, 496:497], 1.0 / nelem, 1e-5,
                     OP.mult, OP.max),
            reads=[rPSM], writes=[rcol["tt2"]])
        one("t", lambda e: e.matmul(PSM[:, 497:498], ONE1[:], COL["tt2"][0:1, :],
                                    start=True, stop=True),
            reads=[rcol["tt2"], cst, rPSM], writes=[rPSM])
        one("v", (lambda swci=swci: lambda e: e.tensor_copy(
            out=COL[swci][:], in_=PSM[:, 497:498]))(),
            reads=[rPSM], writes=[rcol[swci]])
        one("v", (lambda swc=swc, swci=swci: lambda e: e.reciprocal(
            out=COL[swc][:], in_=COL[swci][:]))(),
            reads=[rcol[swci]], writes=[rcol[swc]])

    def wq_quant(idx, src, dst_scr, rdst, swc):
        sl, cw = wpiece(src, idx)
        dsl, _ = wpiece(dst_scr, idx)
        one("y", d_load(WST[:, 0:cw], sl), reads=[rwst], writes=[rwst], inc=16)
        one("s", (lambda swc=swc, cw=cw: lambda e: e.activation(
            out=WST[:, 0:cw], in_=WST[:, 0:cw], func=AF.Copy,
            scale=COL[swc][:], bias=MAGIC))(),
            reads=[rwst, rcol[swc]], writes=[rwst])
        one("v", (lambda cw=cw: ts_(WST[:, 0:cw], WST[:, 0:cw], -MAGIC, 1.0,
                                    OP.add, OP.min))(),
            reads=[rwst], writes=[rwst])
        one("v", (lambda cw=cw: ts_(WQP[:, 0:cw], WST[:, 0:cw], -1.0, None,
                                    OP.max))(),
            reads=[rwst], writes=[rb["WQP"]])
        one("g", (lambda dsl=dsl, cw=cw: d_load(dsl, WQP[:, 0:cw]))(),
            reads=[rb["WQP"]], writes=[rdst], inc=16)

    ACUT = float(_os.environ.get("K_ACUT", "99"))

    def phaseA_front(c):
        if c >= NCH:
            return
        p = c % 2
        xch, xq = XCH[p], XQB[p]
        rx, rxq = rb[f"XCH{p}"], rb[f"XQB{p}"]
        XN32 = XN32P[p]
        cl = lambda n: ACOL[p][n][:]
        rc = lambda n: racol[p][n]

        one("y", d_load(xch[:], dx[c * C:(c + 1) * C, :]),
            reads=[rx], writes=[rx], inc=16)

        if ACUT < 1:
            return
        one("v", lambda e: e.reduce_sum(out=cl("xsum"), in_=xch[:], axis=AX),
            reads=[rx], writes=[rc("xsum")])
        one("s", lambda e: e.activation(out=DUMP[:, E:2 * E], in_=xch[:],
                                        func=AF.Square, accum_out=cl("xss")),
            reads=[rx, rb["DUMPb"]], writes=[rc("xss"), rb["DUMPb"]])
        one("v", lambda e: e.reduce_max(out=cl("xmax"), in_=xch[:], axis=AX),
            reads=[rx], writes=[rc("xmax")])
        one("v", lambda e: e.tensor_reduce(out=cl("xmin"), in_=xch[:],
                                           axis=AX, op=OP.min),
            reads=[rx], writes=[rc("xmin")])
        one("v", ts_(cl("m"), cl("xsum"), 1.0 / E, None, OP.mult),
            reads=[rc("xsum")], writes=[rc("m")])
        one("v", lambda e: e.tensor_mul(cl("msq"), cl("m"), cl("m")),
            reads=[rc("m")], writes=[rc("msq")])
        one("v", lambda e: e.scalar_tensor_tensor(
            out=cl("var"), in0=cl("xss"), scalar=1.0 / E, in1=cl("msq"),
            op0=OP.mult, op1=OP.subtract),
            reads=[rc("xss"), rc("msq")], writes=[rc("var")])
        one("s", lambda e: e.activation(out=cl("lnv"), in_=cl("var"),
                                        func=AF.Ln, bias=C1E5[:]),
            reads=[rc("var")], writes=[rc("lnv")])
        one("s", lambda e: e.activation(out=cl("rs"), in_=cl("lnv"),
                                        func=AF.Exp, scale=-0.5),
            reads=[rc("lnv")], writes=[rc("rs")])
        one("v", lambda e: e.tensor_sub(cl("t1"), cl("xmax"), cl("m")),
            reads=[rc("xmax"), rc("m")], writes=[rc("t1")])
        one("v", lambda e: e.tensor_sub(cl("t2"), cl("m"), cl("xmin")),
            reads=[rc("m"), rc("xmin")], writes=[rc("t2")])
        one("v", lambda e: e.tensor_max(cl("mx"), cl("t1"), cl("t2")),
            reads=[rc("t1"), rc("t2")], writes=[rc("mx")])
        one("v", lambda e: e.reciprocal(out=cl("rmx"), in_=cl("mx")),
            reads=[rc("mx")], writes=[rc("rmx")])
        one("v", ts_(cl("qsc"), cl("rmx"), 127.0, None, OP.mult),
            reads=[rc("rmx")], writes=[rc("qsc")])
        one("v", lambda e: e.tensor_mul(cl("nmq"), cl("m"), cl("qsc")),
            reads=[rc("m"), rc("qsc")], writes=[rc("nmq")])
        one("v", ts_(cl("bq"), cl("nmq"), -1.0, MAGIC, OP.mult, OP.add),
            reads=[rc("nmq")], writes=[rc("bq")])
        one("v", lambda e: e.scalar_tensor_tensor(
            out=cl("bxn"), in0=cl("m"), scalar=-1.0, in1=cl("rs"),
            op0=OP.mult, op1=OP.mult),
            reads=[rc("m"), rc("rs")], writes=[rc("bxn")])
        one("v", ts_(cl("vp"), cl("var"), 1e-5, None, OP.add),
            reads=[rc("var")], writes=[rc("vp")])
        one("v", lambda e: e.reciprocal(out=cl("rvp"), in_=cl("vp")),
            reads=[rc("vp")], writes=[rc("rvp")])
        one("v", lambda e: e.tensor_mul(cl("m2n"), cl("var"), cl("rvp")),
            reads=[rc("var"), rc("rvp")], writes=[rc("m2n")])
        one("s", lambda e: e.activation(out=cl("l2"), in_=cl("m2n"),
                                        func=AF.Ln, bias=C1E6[:]),
            reads=[rc("m2n")], writes=[rc("l2")])
        one("s", lambda e: e.activation(out=cl("r2"), in_=cl("l2"),
                                        func=AF.Exp, scale=-0.5),
            reads=[rc("l2")], writes=[rc("r2")])
        one("v", lambda e: e.tensor_mul(cl("u1"), cl("r2"), cl("rs")),
            reads=[rc("r2"), rc("rs")], writes=[rc("u1")])
        one("v", lambda e: e.tensor_mul(cl("u2"), cl("u1"), cl("mx")),
            reads=[rc("u1"), rc("mx")], writes=[rc("u2")])
        one("v", ts_(SXI[:, c:c + 1], cl("u2"), 1e-5, 1.0 / 127.0,
                     OP.max, OP.mult),
            reads=[rc("u2")], writes=[rb["SXI"]])

        one("v", lambda e: e.scalar_tensor_tensor(
            out=XN32[:], in0=xch[:], scalar=cl("rs"),
            in1=cl("bxn").broadcast_to([128, E]),
            op0=OP.mult, op1=OP.add),
            reads=[rx, rc("rs"), rc("bxn")], writes=[rb[f"XN32_{p}"]])
        if ACUT < 2:
            return
        one("v", lambda e: e.scalar_tensor_tensor(
            out=TRND[:], in0=xch[:], scalar=cl("qsc"),
            in1=cl("bq").broadcast_to([128, E]),
            op0=OP.mult, op1=OP.add),
            reads=[rx, rc("qsc"), rc("bq"), rb["TRND"]], writes=[rb["TRND"]])
        one("v", (lambda xq=xq: lambda e: e.tensor_scalar(
            out=xq[:], in0=TRND[:], scalar1=-MAGIC, scalar2=None, op0=OP.add))(),
            reads=[rb["TRND"]], writes=[rxq])
        one("g", d_load(sXq[c * C:(c + 1) * C, :], xq[:]),
            reads=[rxq], writes=[rb["scrXQ"]], inc=16)

    def phaseA_chunk(c):
        p = c % 2
        xch = XCH[p]
        rx = rb[f"XCH{p}"]
        gb, atb = GB[p], ATB[p]
        rgb, ratb = rb[f"GB{p}"], rb[f"ATB{p}"]
        XN32 = XN32P[p]
        rxn = rb[f"XN32_{p}"]

        do("t", [(lambda i=i: lambda e: e.transpose(
            PQS[:, i * 128:(i + 1) * 128], XN32[:, i * 128:(i + 1) * 128],
            IDF[:]))() for i in range(8)],
            reads=[rxn, cst, rPQS], writes=[rPQS])
        one("v", lambda e: e.tensor_copy(out=XNF[:], in_=PQS[:]),
            reads=[rPQS], writes=[rb["XNF"]])

        if ACUT < 3:
            return
        fns = [(lambda kt=kt: lambda e: e.matmul(
            PSM[:, 32:48], XNF[:, kt * 128:(kt + 1) * 128],
            WAB[:, kt * 16:(kt + 1) * 16], start=(kt == 0), stop=False))()
            for kt in range(8)]
        fns.append(lambda e: e.matmul(PSM[:, 32:48], ONE1[:], DTB[:],
                                      start=False, stop=True))
        do("t", fns, reads=[rb["WABw"], rb["XNF"], cst, rPSM], writes=[rPSM])
        one("s", lambda e: e.activation(out=N16[:], in_=PSM[:, 32:48],
                                        func=AF.Exp),
            reads=[rPSM], writes=[rb["N16"]])
        one("s", lambda e: e.activation(out=N16[:], in_=N16[:], func=AF.Ln,
                                        bias=1.0),
            reads=[rb["N16"]], writes=[rb["N16"]])
        one("v", lambda e: e.tensor_mul(N16[:], N16[:], NEAB[:]),
            reads=[rb["N16"], rb["NEAB"]], writes=[rb["N16"]])
        one("s", lambda e: e.activation(out=N16[:], in_=N16[:], func=AF.Exp),
            reads=[rb["N16"]], writes=[rb["N16"]])
        one("s", lambda e: e.activation(out=LA[:], in_=N16[:], func=AF.Ln,
                                        bias=C1E8[:]),
            reads=[rb["N16"]], writes=[rb["LA"]])
        fns = []
        for m in range(8):
            for kt in range(8):
                fns.append((lambda m=m, kt=kt: lambda e: e.matmul(
                    PK[:, m * 128:(m + 1) * 128],
                    WK[:, kt * E + m * 128: kt * E + (m + 1) * 128],
                    XNF[:, kt * 128:(kt + 1) * 128],
                    start=(kt == 0), stop=(kt == 7)))())
        do("t", fns, reads=[rb["WKw"], rb["XNF"], rPK], writes=[rPK])
        if ACUT < 3.2:
            return
        one("v", lambda e: e.tensor_copy(out=KB[:], in_=PK[:]),
            reads=[rPK], writes=[rb["KB"]])
        one("v", lambda e: e.tensor_copy(out=KHI[64:128, :], in_=PK[64:128, :]),
            reads=[rPK], writes=[rb["KHI"]])
        one("s", lambda e: e.activation(out=KSQ[:], in_=KB[:], func=AF.Square),
            reads=[rb["KB"]], writes=[rb["KSQ"]])
        if ACUT < 3.4:
            return
        fns = []
        for m in range(8):
            for kt in range(8):
                fns.append((lambda m=m, kt=kt: lambda e: e.matmul(
                    PQS[:, m * 128:(m + 1) * 128],
                    WQ[:, kt * E + m * 128: kt * E + (m + 1) * 128],
                    XNF[:, kt * 128:(kt + 1) * 128],
                    start=(kt == 0), stop=(kt == 7)))())
        do("t", fns, reads=[rb["WQw"], rb["XNF"], rPQS], writes=[rPQS])
        one("v", lambda e: e.tensor_copy(out=QF[:], in_=PQS[:]),
            reads=[rPQS], writes=[rb["QF"]])
        one("s", lambda e: e.activation(out=QSQ[:], in_=QF[:], func=AF.Square),
            reads=[rb["QF"]], writes=[rb["QSQ"]])
        if ACUT < 3.6:
            return
        fns = []
        for ns in range(2):
            for kt in range(8):
                fns.append((lambda ns=ns, kt=kt: lambda e: e.matmul(
                    PVG[:, ns * 512:(ns + 1) * 512],
                    XNF[:, kt * 128:(kt + 1) * 128],
                    WV[:, kt * E + ns * 512: kt * E + (ns + 1) * 512],
                    start=(kt == 0), stop=(kt == 7)))())
        do("t", fns, reads=[rb["WVw"], rb["XNF"], rPVG], writes=[rPVG])
        one("v", lambda e: e.tensor_copy(out=VB[:], in_=PVG[:]),
            reads=[rPVG], writes=[rb["VB"]])
        if ACUT < 3.8:
            return
        fns = []
        for ns in range(2):
            for kt in range(8):
                fns.append((lambda ns=ns, kt=kt: lambda e: e.matmul(
                    PVG[:, ns * 512:(ns + 1) * 512],
                    XNF[:, kt * 128:(kt + 1) * 128],
                    WG[:, kt * E + ns * 512: kt * E + (ns + 1) * 512],
                    start=(kt == 0), stop=(kt == 7)))())
        do("t", fns, reads=[rb["WGw"], rb["XNF"], rPVG], writes=[rPVG])
        one("v", (lambda gb=gb: lambda e: e.tensor_copy(out=gb[:], in_=PVG[:]))(),
            reads=[rPVG], writes=[rgb])
        one("g", d_load(sG[c * C:(c + 1) * C, :], gb[:]),
            reads=[rgb], writes=[rb["scrG"]], inc=16)

        one("t", lambda e: e.matmul(PSM[:, 48:64], TRI[:], LA[:],
                                    start=True, stop=True),
            reads=[cst, rb["LA"], rPSM], writes=[rPSM])
        one("v", lambda e: e.tensor_copy(out=CUMT[:], in_=PSM[:, 48:64]),
            reads=[rPSM], writes=[rb["CUMT"]])
        one("t", lambda e: e.matmul(PSM[0:16, 64:192], LA[:], TRI[:],
                                    start=True, stop=True),
            reads=[cst, rb["LA"], rPSM], writes=[rPSM])
        one("v", lambda e: e.tensor_copy(out=CUMF[:], in_=PSM[0:16, 64:192]),
            reads=[rPSM], writes=[rb["CUMF"]])
        one("g", d_load(GROW[:], CUMT[127:128, :]),
            reads=[rb["CUMT"]], writes=[rb["GROW"]], inc=16)

        if ACUT < 5:
            return

        phaseA_front(c + 1)

        if ACUT < 4:
            return
        fns = [(lambda f=f: lambda e: e.matmul(
            PSM[0:16, 192:320], BLKH[:, f * 16:(f + 1) * 16],
            QSQ[:, f * 128:(f + 1) * 128], start=(f == 0), stop=(f == 7)))()
            for f in range(8)]
        do("t", fns, reads=[cst, rb["QSQ"], rPSM], writes=[rPSM])
        one("v", ts_(T16A[:], PSM[0:16, 192:320], 1e-24, None, OP.max),
            reads=[rPSM], writes=[rb["T16A"]])
        one("s", lambda e: e.activation(out=T16A[:], in_=T16A[:], func=AF.Ln),
            reads=[rb["T16A"]], writes=[rb["T16A"]])
        one("v", lambda e: e.scalar_tensor_tensor(
            out=UFM[:], in0=T16A[:], scalar=-0.5, in1=CUMF[:],
            op0=OP.mult, op1=OP.add),
            reads=[rb["T16A"], rb["CUMF"]], writes=[rb["UFM"]])
        one("s", lambda e: e.activation(out=EUS[:], in_=UFM[:], func=AF.Exp),
            reads=[rb["UFM"]], writes=[rb["EUS"]])
        one("v", lambda e: e.tensor_copy(out=EUHI[:], in_=EUS[:]),
            reads=[rb["EUS"]], writes=[rb["EUHI"]])
        one("v", lambda e: e.tensor_copy(out=T16B[:], in_=EUHI[:]),
            reads=[rb["EUHI"]], writes=[rb["T16B"]])
        one("v", lambda e: e.tensor_sub(EULO[:], EUS[:], T16B[:]),
            reads=[rb["EUS"], rb["T16B"]], writes=[rb["EULO"]])
        one("g", d_load(UFLAT[:], UFM[:]),
            reads=[rb["UFM"]], writes=[rb["UFLAT"]], inc=16)

        fns = [(lambda h=h: lambda e: e.matmul(
            PSM[:, 320 + h:321 + h],
            KSQ[:, (h // 2) * 128:(h // 2 + 1) * 128],
            ONP[:, (h % 2):(h % 2) + 1], start=True, stop=True))()
            for h in range(16)]
        do("t", fns, reads=[cst, rb["KSQ"], rPSM], writes=[rPSM])
        one("v", ts_(N16[:], PSM[:, 320:336], 1e-24, None, OP.max),
            reads=[rPSM, rb["N16"]], writes=[rb["N16"]])
        one("s", lambda e: e.activation(out=N16[:], in_=N16[:], func=AF.Ln),
            reads=[rb["N16"]], writes=[rb["N16"]])
        one("v", lambda e: e.scalar_tensor_tensor(
            out=WNT[:], in0=N16[:], scalar=-0.5, in1=CUMT[:],
            op0=OP.mult, op1=OP.subtract),
            reads=[rb["N16"], rb["CUMT"]], writes=[rb["WNT"]])

        fns = [(lambda f=f: lambda e: e.matmul(
            PSM[0:16, 336:464], BLKH[:, f * 16:(f + 1) * 16],
            KSQ[:, f * 128:(f + 1) * 128], start=(f == 0), stop=(f == 7)))()
            for f in range(8)]
        do("t", fns, reads=[cst, rb["KSQ"], rPSM], writes=[rPSM])
        one("v", ts_(T16B[:], PSM[0:16, 336:464], 1e-24, None, OP.max),
            reads=[rPSM, rb["T16B"]], writes=[rb["T16B"]])
        one("s", lambda e: e.activation(out=T16B[:], in_=T16B[:], func=AF.Ln),
            reads=[rb["T16B"]], writes=[rb["T16B"]])
        one("v", lambda e: e.scalar_tensor_tensor(
            out=WNF[:], in0=T16B[:], scalar=-0.5, in1=CUMF[:],
            op0=OP.mult, op1=OP.subtract),
            reads=[rb["T16B"], rb["CUMF"]], writes=[rb["WNF"]])
        one("s", lambda e: e.activation(out=EWS[:], in_=WNF[:], func=AF.Exp,
                                        bias=CUMF[:, 127:128]),
            reads=[rb["WNF"], rb["CUMF"]], writes=[rb["EWS"]])
        one("v", lambda e: e.tensor_copy(out=EWHI[:], in_=EWS[:]),
            reads=[rb["EWS"]], writes=[rb["EWHI"]])
        one("v", lambda e: e.tensor_copy(out=T16B[:], in_=EWHI[:]),
            reads=[rb["EWHI"], rb["T16B"]], writes=[rb["T16B"]])
        one("v", lambda e: e.tensor_sub(EWLO[:], EWS[:], T16B[:]),
            reads=[rb["EWS"], rb["T16B"]], writes=[rb["EWLO"]])

        one("s", lambda e: e.activation(out=EGROW[:], in_=GROW[:], func=AF.Exp),
            reads=[rb["GROW"], rb["EGROW"]], writes=[rb["EGROW"]])
        one("t", lambda e: e.matmul(PSM[0:64, 464:480], ONE1[:, 0:64],
                                    EGROW[:], start=True, stop=True),
            reads=[cst, rb["EGROW"], rPSM], writes=[rPSM])
        one("v", lambda e: e.tensor_copy(out=EGC[:], in_=PSM[0:64, 464:480]),
            reads=[rPSM], writes=[rb["EGC"]])

        if ACUT < 6:
            return
        fns = []
        for f in range(8):
            for j, src in ((0, EUHI), (1, EULO)):
                fns.append((lambda f=f, j=j, src=src: lambda e: e.matmul(
                    PVG[:, f * 128:(f + 1) * 128],
                    BLKHT[:, f * 128:(f + 1) * 128], src[:],
                    start=(j == 0), stop=(j == 1)))())
        do("t", fns, reads=[cst, rb["EUHI"], rb["EULO"], rPVG], writes=[rPVG])
        one("v", lambda e: e.tensor_copy(out=EUW[:], in_=PVG[:]),
            reads=[rPVG], writes=[rb["EUW"]])
        one("v", lambda e: e.tensor_mul(QTF[:], PQS[:], EUW[:]),
            reads=[rPQS, rb["EUW"]], writes=[rb["QTF"]])
        one("v", lambda e: e.tensor_mul(QTH[64:128, :], PQS[64:128, :],
                                        EUW[64:128, :]),
            reads=[rPQS, rb["EUW"]], writes=[rb["QTH"]])
        fns = []
        for f in range(8):
            for j, src in ((0, EWHI), (1, EWLO)):
                fns.append((lambda f=f, j=j, src=src: lambda e: e.matmul(
                    PVG[:, f * 128:(f + 1) * 128],
                    BLKHT[:, f * 128:(f + 1) * 128], src[:],
                    start=(j == 0), stop=(j == 1)))())
        do("t", fns, reads=[cst, rb["EWHI"], rb["EWLO"], rPVG], writes=[rPVG])
        one("v", lambda e: e.tensor_copy(out=EUW[:], in_=PVG[:]),
            reads=[rPVG, rb["EUW"]], writes=[rb["EUW"]])
        one("v", lambda e: e.tensor_mul(KHF[:], PK[:], EUW[:]),
            reads=[rPK, rb["EUW"]], writes=[rb["KHF"]])
        do("t", [(lambda i=i: lambda e: e.transpose(
            PSM2.bitcast(BF16)[:, i * 128:(i + 1) * 128],
            KHF[:, i * 128:(i + 1) * 128],
            IDB[:]))() for i in range(8)],
            reads=[rb["KHF"], cst, rPSM2], writes=[rPSM2])
        one("v", lambda e: e.tensor_copy(out=KHT[:], in_=PSM2.bitcast(BF16)[:]),
            reads=[rPSM2], writes=[rb["KHT"]])

        if ACUT < 7:
            return
        for r in range(2):
            fns = []
            for s_ in range(8):
                h = 8 * r + s_
                f = h // 2
                if h % 2 == 0:
                    fns.append((lambda s_=s_, f=f: lambda e: e.matmul(
                        PQS[:, s_ * 128:(s_ + 1) * 128],
                        KB[0:64, f * 128:(f + 1) * 128],
                        QF[0:64, f * 128:(f + 1) * 128],
                        start=True, stop=True))())
                else:
                    fns.append((lambda s_=s_, f=f: lambda e: e.matmul(
                        PQS[:, s_ * 128:(s_ + 1) * 128],
                        KHI[:, f * 128:(f + 1) * 128],
                        QF[:, f * 128:(f + 1) * 128],
                        start=True, stop=True))())
            do("t", fns, reads=[rb["KB"], rb["KHI"], rb["QF"], rPQS],
               writes=[rPQS])
            for b4 in range(2):
                h0 = 8 * r + 4 * b4
                PU = PSM if b4 == 0 else PSM2
                rU = rPSM if b4 == 0 else rPSM2
                fns = []
                for j in range(4):
                    h = h0 + j
                    fns.append((lambda PU=PU, j=j, h=h: lambda e: e.matmul(
                        PU[:, j * 128:(j + 1) * 128], ONE1[:],
                        UFLAT[0:1, h * 128:(h + 1) * 128],
                        start=True, stop=False))())
                    fns.append((lambda PU=PU, j=j: lambda e: e.matmul(
                        PU[:, j * 128:(j + 1) * 128], IDB[:], MSKN[:],
                        start=False, stop=True))())
                do("t", fns, reads=[cst, rb["UFLAT"], rU], writes=[rU])
                eb_toks = []
                for j in range(4):
                    h = h0 + j
                    one("s", (lambda PU=PU, j=j, h=h: lambda e: e.activation(
                        out=EBF[j][:], in_=PU[:, j * 128:(j + 1) * 128],
                        func=AF.Exp, bias=WNT[:, h:h + 1]))(),
                        reads=[rU, rb["WNT"], rb[f"EBF{j}"]],
                        writes=[rb[f"EBF{j}"]])
                for j in range(4):
                    h = h0 + j
                    s_ = h - 8 * r
                    one("v", (lambda j=j, s_=s_: lambda e: e.tensor_mul(
                        PBF[j][:], PQS[:, s_ * 128:(s_ + 1) * 128],
                        EBF[j][:]))(),
                        reads=[rPQS, rb[f"EBF{j}"], rb[f"PBF{j}"]],
                        writes=[rb[f"PBF{j}"]])
                fns = []
                for j in range(4):
                    h = h0 + j
                    f = h // 2
                    fns.append((lambda j=j, h=h: lambda e: e.matmul(
                        PVG[:, h * 64:(h + 1) * 64], PBF[j][:],
                        VB[:, h * 64:(h + 1) * 64], start=True, stop=False))())
                    if h % 2 == 0:
                        fns.append((lambda h=h, f=f: lambda e: e.matmul(
                            PVG[:, h * 64:(h + 1) * 64],
                            QTF[0:64, f * 128:(f + 1) * 128],
                            MS128[0:64, h * 64:(h + 1) * 64],
                            start=False, stop=True))())
                    else:
                        fns.append((lambda h=h, f=f: lambda e: e.matmul(
                            PVG[:, h * 64:(h + 1) * 64],
                            QTH[:, f * 128:(f + 1) * 128],
                            MS128[:, h * 64:(h + 1) * 64],
                            start=False, stop=True))())
                    fns.append((lambda h=h: lambda e: e.matmul(
                        PK[0:64, h * 64:(h + 1) * 64],
                        KHT[:, h * 64:(h + 1) * 64],
                        VB[:, h * 64:(h + 1) * 64], start=True, stop=True))())
                do("t", fns, reads=[rb["PBF0"], rb["PBF1"], rb["PBF2"],
                                    rb["PBF3"], rb["VB"], rb["QTF"],
                                    rb["QTH"], rb["MS128"], rb["KHT"],
                                    rPVG, rPK],
                   writes=[rPVG, rPK])

        one("v", lambda e: e.tensor_tensor(
            out=MTMP[:], in0=MSB[:],
            in1=EGC[:].unsqueeze(2).broadcast_to([64, 16, 64]),
            op=OP.mult),
            reads=[rb["MSB"], rb["EGC"], rb["MTMP"]], writes=[rb["MTMP"]])
        one("v", lambda e: e.tensor_add(MSB[:], PK[0:64, :], MTMP[:]),
            reads=[rPK, rb["MTMP"], rb["MSB"]], writes=[rb["MSB"]])
        one("v", lambda e: e.tensor_copy(out=MS128[0:64, :], in_=MSB[:]),
            reads=[rb["MSB"], rb["MS128"]], writes=[rb["MS128"]])
        one("g", d_load(MS128[64:128, :], MS128[0:64, :]),
            reads=[rb["MS128"]], writes=[rb["MS128"]], inc=16)

        if ACUT < 9:
            return
        one("v", lambda e: e.tensor_copy(out=XN32[:], in_=PVG[:]),
            reads=[rPVG, rxn], writes=[rxn])
        one("s", lambda e: e.activation(out=DUMP[:, 0:E], in_=XN32[:],
                                        func=AF.Square),
            reads=[rxn, rb["DUMPa"]], writes=[rb["DUMPa"]])
        one("v", lambda e: e.tensor_reduce(
            out=NSA[:], in_=DUMP[:, 0:E].rearrange("p (g k) -> p g k", k=64),
            axis=AX, op=OP.add),
            reads=[rb["DUMPa"]], writes=[rb["NSA"]])
        one("v", ts_(NSA[:], NSA[:], 1.0 / 64.0, 1e-6, OP.mult, OP.add),
            reads=[rb["NSA"]], writes=[rb["NSA"]])
        one("s", lambda e: e.activation(out=NSA[:], in_=NSA[:], func=AF.Ln),
            reads=[rb["NSA"]], writes=[rb["NSA"]])
        one("s", lambda e: e.activation(out=SATT[:], in_=NSA[:], func=AF.Exp,
                                        scale=-0.5),
            reads=[rb["NSA"]], writes=[rb["SATT"]])
        one("v", (lambda atb=atb: lambda e: e.tensor_tensor(
            out=atb[:], in0=XN32[:],
            in1=SATT[:].unsqueeze(2).broadcast_to([128, 16, 64]),
            op=OP.mult))(),
            reads=[rxn, rb["SATT"], ratb], writes=[ratb])
        one("g", d_load(sAtt[c * C:(c + 1) * C, :], atb[:]),
            reads=[ratb], writes=[rb["scrATT"]], inc=16)

    phaseA_front(0)
    if NCH < 32:
        for idx in range(16):
            wq_abs(idx, dW1t, idx)
        wq_mean("swc1", "swc1i", E * FF, 16)
        for idx in range(16):
            wq_quant(idx, dW1t, sW1q, rb["scrW1"], "swc1")
        for idx in range(32):
            wq_abs(idx, dW2t, idx)
        wq_mean("swc2", "swc2i", E * FF, 32)
        for idx in range(32):
            wq_quant(idx, dW2t, sW2q, rb["scrW2"], "swc2")
    for c in range(NCH):
        phaseA_chunk(c)
        if NCH < 32:
            continue
        if c < 8:
            wq_abs(2 * c, dW1t, 2 * c)
            wq_abs(2 * c + 1, dW1t, 2 * c + 1)
        elif c < 16:
            if c == 8:
                wq_mean("swc1", "swc1i", E * FF, 16)
            wq_quant(2 * (c - 8), dW1t, sW1q, rb["scrW1"], "swc1")
            wq_quant(2 * (c - 8) + 1, dW1t, sW1q, rb["scrW1"], "swc1")
        elif c < 24:
            for j in range(4):
                wq_abs(4 * (c - 16) + j, dW2t, 4 * (c - 16) + j)
        else:
            if c == 24:
                wq_mean("swc2", "swc2i", E * FF, 32)
            for j in range(4):
                wq_quant(4 * (c - 24) + j, dW2t, sW2q, rb["scrW2"], "swc2")

    if _os.environ.get("K_DBG"):
        for cn in ["xsum", "xss", "xmax", "xmin", "m", "var", "rs", "bxn",
                   "qsc", "bq", "mx", "msq", "t1", "t2", "rmx", "nmq"]:
            dob = nc.dram_tensor("dbgc_" + cn, [128, 1], F32,
                                 kind="ExternalOutput").ap()
            one("y", d_load(dob, COL[cn][:]), reads=[rcol[cn]],
                writes=[rb["yst"]], inc=16)
        dbgs = [("UFLAT", UFLAT, [1, 2048], F32), ("WNT", WNT, [128, 16], F32),
                ("LA", LA, [128, 16], F32), ("CUMT", CUMT, [128, 16], F32),
                ("CUMF", CUMF, [16, 128], F32), ("UFM", UFM, [16, 128], F32),
                ("KB", KB, [128, E], BF16), ("QF", QF, [128, E], BF16),
                ("ATB0", ATB[0], [128, E], BF16), ("EBF0", EBF[0], [128, 128], BF16),
                ("PBF0", PBF[0], [128, 128], BF16), ("VB", VB, [128, E], BF16),
                ("MS", MS128, [128, E], BF16), ("XNF", XNF, [128, E], BF16),
                ("KHT", KHT, [128, E], BF16), ("QTF", QTF, [128, E], BF16)]
        for nm, buf, shp, dt_ in dbgs:
            dob = nc.dram_tensor("dbg_" + nm, shp, dt_, kind="ExternalOutput").ap()
            one("y", d_load(dob, buf[:]), reads=[rb.get(nm) or R()],
                writes=[rb["yst"]], inc=16)

    barrier()
    esA.close()

    esB = contextlib.ExitStack()
    sbB = lambda n, s, dt=F32: esB.enter_context(nc.sbuf_tensor(n, s, dt))
    psB_ = lambda n, s, dt=F32: esB.enter_context(nc.psum_tensor(n, s, dt))

    W1Q = sbB("W1Q", [128, 8 * FF], BF16)
    W2Q = sbB("W2Q", [128, 32 * E], BF16)
    ATBB = [sbB("ATBB0", [128, E], BF16)] * 2
    GBB = [sbB("GBB0", [128, E], BF16)] * 2
    XRES = [sbB("XRES0", [128, E])] * 2
    XQF = sbB("XQF", [128, E], BF16)
    HG = sbB("HG", [128, FF])
    HQS = [sbB("HQS0", [128, 512], BF16)] * 2
    HQF = [sbB("HQF0", [128, E], BF16)] * 2
    GT = sbB("GT", [128, E])
    GT2 = sbB("GT2", [128, E], BF16)
    ATG = sbB("ATG", [128, E], BF16)
    ATGF = ATG
    YOUT = [GT] * 2
    for name in ["W1Qw", "W2Qw", "ATBB0", "GBB0", "XRES0",
                 "XQF", "HG", "HQS0", "HQF0", "GT",
                 "GT2", "ATG", "ATGF"]:
        RR(name)
    rb["ATBB1"] = rb["ATBB0"]; rb["GBB1"] = rb["GBB0"]
    rb["XRES1"] = rb["XRES0"]; rb["HQS1"] = rb["HQS0"]
    rb["HQF1"] = rb["HQF0"]; rb["ATGF"] = rb["ATG"]
    rb["YOUT0"] = rb["GT"]; rb["YOUT1"] = rb["GT"]

    PH1 = psB_("PH1", [128, 512]); PH2 = psB_("PH2", [128, 512])
    PY = psB_("PY", [128, 1024])
    PYW = psB_("PYW", [128, 1024])
    PB_TB1 = psB_("PB_TB1", [128, 1024], BF16)
    PB_TB2 = psB_("PB_TB2", [128, 1024], BF16)
    PB_TR = PB_TB1
    rPH = [R(), R()]
    rPY, rPYW, rPTB = R(), R(), [R(), R()]
    rPTR = rPTB[0]
    ONESB = sbB("ONESB", [1, 128], BF16)
    BPART = sbB("BPART", [128, 16])
    rb["ONESB"] = R(); rb["BPART"] = R()
    one("v", lambda e: e.memset(ONESB[:], 1.0), writes=[rb["ONESB"]])

    do("y", [(lambda i=i: d_load(W1Q[:, i * FF:(i + 1) * FF],
                                 sW1q[i * 128:(i + 1) * 128, :]))()
             for i in range(8)],
       reads=[rb["scrW1"]], writes=[rb["W1Qw"]], inc=16, inc_each=True)
    do("y", [(lambda i=i: d_load(W2Q[:, i * E:(i + 1) * E],
                                 sW2q[i * 128:(i + 1) * 128, :]))()
             for i in range(32)],
       reads=[rb["scrW2"]], writes=[rb["W2Qw"]], inc=16, inc_each=True)

    def phaseB_chunk(c):
        p = c % 2
        cl = lambda n: COL[n][:]
        rc = lambda n: rcol[n]
        atb, gbb, xres, yout = ATBB[p], GBB[p], XRES[p], YOUT[p]
        ratb, rgbb = rb[f"ATBB{p}"], rb[f"GBB{p}"]
        rxres, ryout = rb[f"XRES{p}"], rb[f"YOUT{p}"]

        one("y", d_load(atb[:], sAtt[c * C:(c + 1) * C, :]),
            reads=[rb["scrATT"], ratb], writes=[ratb], inc=16)
        one("y", d_load(gbb[:], sG[c * C:(c + 1) * C, :]),
            reads=[rb["scrG"], rgbb], writes=[rgbb], inc=16)
        one("y", d_load(xres[:], dx[c * C:(c + 1) * C, :]),
            reads=[rxres], writes=[rxres], inc=16)
        do("y", [(lambda i=i: lambda e: e.dma_start_transpose(
            out=XQF[:, i * 128:(i + 1) * 128],
            in_=sXq[c * C:(c + 1) * C, i * 128:(i + 1) * 128]))()
            for i in range(8)],
            reads=[rb["scrXQ"], rb["XQF"]], writes=[rb["XQF"]], inc=16,
           inc_each=True)

        one("v", lambda e: e.tensor_mul(cl("a1"), SXI[:, c:c + 1],
                                        COL["swc1i"][:]),
            reads=[rb["SXI"], rcol["swc1i"]], writes=[rc("a1")])
        one("v", lambda e: e.reciprocal(out=cl("ra1"), in_=cl("a1")),
            reads=[rc("a1")], writes=[rc("ra1")])
        one("g", d_load(RAROW[:], COL["ra1"][:]),
            reads=[rc("ra1"), rb["RAROW"]], writes=[rb["RAROW"]], inc=16)
        one("v", lambda e: e.tensor_copy(out=RAROWB[:], in_=RAROW[:]),
            reads=[rb["RAROW"], rb["RAROWB"]], writes=[rb["RAROWB"]])

        for os in range(8):
            pp = os % 2
            fns = [(lambda os=os, kt=kt, pp=pp: lambda e: e.matmul(
                (PH1 if pp == 0 else PH2)[:, 0:512],
                XQF[:, kt * 128:(kt + 1) * 128],
                W1Q[:, kt * FF + os * 512: kt * FF + (os + 1) * 512],
                start=(kt == 0), stop=False))() for kt in range(8)]
            fns.append((lambda os=os, pp=pp: lambda e: e.matmul(
                (PH1 if pp == 0 else PH2)[:, 0:512], RAROWB[:],
                B1B[0:1, os * 512:(os + 1) * 512],
                start=False, stop=True))())
            do("t", fns, reads=[rb["W1Qw"], rb["XQF"], rb["RAROWB"],
                                rb["B1Bw"], rPH[pp]], writes=[rPH[pp]])
            one("s", (lambda os=os, pp=pp: lambda e: e.activation(
                out=HG[:, os * 512:(os + 1) * 512],
                in_=(PH1 if pp == 0 else PH2)[:, 0:512],
                func=AF.Gelu, scale=cl("a1")))(),
                reads=[rPH[pp], rc("a1"), rb["HG"]], writes=[rb["HG"]])
            one("v", (lambda os=os: lambda e: e.reduce_max(
                out=BPART[:, os:os + 1], in_=HG[:, os * 512:(os + 1) * 512],
                axis=AX, apply_absolute_value=True))(),
                reads=[rb["HG"], rb["BPART"]], writes=[rb["BPART"]])
            one("s", (lambda os=os: lambda e: e.activation(
                out=DUMP[:, os * 512:(os + 1) * 512],
                in_=HG[:, os * 512:(os + 1) * 512], func=AF.Square,
                accum_out=BPART[:, 8 + os:9 + os]))(),
                reads=[rb["HG"], rb["DUMPa"], rb["DUMPb"], rb["DUMPc"],
                       rb["BPART"]],
                writes=[rb["BPART"], rb["DUMPa"], rb["DUMPb"], rb["DUMPc"]])

        one("s", (lambda gbb=gbb: lambda e: e.activation(
            out=GT[:], in_=gbb[:], func=AF.Tanh, scale=0.5))(),
            reads=[rgbb, rb["GT"]], writes=[rb["GT"]])
        one("v", (lambda gbb=gbb: lambda e: e.scalar_tensor_tensor(
            out=GT2[:], in0=GT[:], scalar=1.0, in1=gbb[:],
            op0=OP.add, op1=OP.mult))(),
            reads=[rb["GT"], rgbb, rb["GT2"]], writes=[rb["GT2"]])
        one("v", (lambda atb=atb: lambda e: e.tensor_mul(ATG[:], atb[:],
                                                         GT2[:]))(),
            reads=[ratb, rb["GT2"], rb["ATG"]], writes=[rb["ATG"]])
        do("t", [(lambda i=i: lambda e: e.transpose(
            PB_TR[:, i * 128:(i + 1) * 128], ATG[:, i * 128:(i + 1) * 128],
            IDB[:]))() for i in range(8)],
            reads=[rb["ATG"], cst, rPTR], writes=[rPTR])
        one("v", lambda e: e.tensor_copy(out=ATGF[:], in_=PB_TR[:]),
            reads=[rPTR, rb["ATGF"]], writes=[rb["ATGF"]])
        fns = []
        for ns in range(2):
            fns.append((lambda ns=ns: lambda e: e.matmul(
                PYW[:, ns * 512:(ns + 1) * 512], ONESB[:],
                B2B[0:1, ns * 512:(ns + 1) * 512],
                start=True, stop=False))())
            for kt in range(8):
                fns.append((lambda ns=ns, kt=kt: lambda e: e.matmul(
                    PYW[:, ns * 512:(ns + 1) * 512],
                    ATGF[:, kt * 128:(kt + 1) * 128],
                    WO[:, kt * E + ns * 512: kt * E + (ns + 1) * 512],
                    start=False, stop=(kt == 7)))())
        do("t", fns, reads=[rb["ATGF"], rb["WOw"], rb["ONESB"], rb["B2Bw"],
                            rPYW], writes=[rPYW])

        one("v", lambda e: e.reduce_max(out=cl("mxh"), in_=BPART[:, 0:8],
                                        axis=AX),
            reads=[rb["BPART"]], writes=[rc("mxh")])
        one("v", lambda e: e.reduce_sum(out=cl("ssh"), in_=BPART[:, 8:16],
                                        axis=AX),
            reads=[rb["BPART"]], writes=[rc("ssh")])
        one("v", ts_(cl("m2h"), cl("ssh"), 1.0 / FF, 1e-6, OP.mult, OP.add),
            reads=[rc("ssh")], writes=[rc("m2h")])
        one("s", lambda e: e.activation(out=cl("sqm"), in_=cl("m2h"),
                                        func=AF.Sqrt),
            reads=[rc("m2h")], writes=[rc("sqm")])
        one("v", lambda e: e.reciprocal(out=cl("r2h"), in_=cl("sqm")),
            reads=[rc("sqm")], writes=[rc("r2h")])
        one("v", ts_(cl("smx"), cl("mxh"), 1e-20, None, OP.max),
            reads=[rc("mxh")], writes=[rc("smx")])
        one("v", lambda e: e.reciprocal(out=cl("rm2"), in_=cl("smx")),
            reads=[rc("smx")], writes=[rc("rm2")])
        one("v", ts_(cl("qs2"), cl("rm2"), 127.0, None, OP.mult),
            reads=[rc("rm2")], writes=[rc("qs2")])
        one("v", lambda e: e.tensor_mul(cl("tt1"), cl("r2h"), cl("mxh")),
            reads=[rc("r2h"), rc("mxh")], writes=[rc("tt1")])
        one("v", ts_(cl("tt2"), cl("tt1"), 1e-5, 1.0 / 127.0, OP.max, OP.mult),
            reads=[rc("tt1")], writes=[rc("tt2")])
        one("v", lambda e: e.tensor_mul(cl("a2"), cl("tt2"), COL["swc2i"][:]),
            reads=[rc("tt2"), rcol["swc2i"]], writes=[rc("a2")])


        for q4 in range(4):
            pp = q4 % 2
            rHq, rHf = rb[f"HQS{pp}"], rb[f"HQF{pp}"]
            PT, rPT = (PB_TB1, rPTB[0]) if pp == 0 else (PB_TB2, rPTB[1])
            for half in range(2):
                one("s", (lambda q4=q4, half=half: lambda e: e.activation(
                    out=HG[:, q4 * 1024 + half * 512:
                           q4 * 1024 + (half + 1) * 512],
                    in_=HG[:, q4 * 1024 + half * 512:
                           q4 * 1024 + (half + 1) * 512],
                    func=AF.Copy, scale=cl("qs2"), bias=MAGIC))(),
                    reads=[rb["HG"], rc("qs2")], writes=[rb["HG"]])
                one("v", (lambda q4=q4, half=half, pp=pp:
                          lambda e: e.tensor_scalar(
                              out=HQS[pp][:, 0:512],
                              in0=HG[:, q4 * 1024 + half * 512:
                                      q4 * 1024 + (half + 1) * 512],
                              scalar1=-MAGIC, scalar2=None, op0=OP.add))(),
                    reads=[rb["HG"], rHq], writes=[rHq])
                do("t", [(lambda i=i, half=half, pp=pp, PT=PT:
                          lambda e: e.transpose(
                              PT[:, (half * 4 + i) * 128:
                                 (half * 4 + i + 1) * 128],
                              HQS[pp][:, i * 128:(i + 1) * 128], IDB[:]))()
                         for i in range(4)],
                   reads=[rHq, cst, rPT], writes=[rPT])
            one("v", (lambda pp=pp, PT=PT: lambda e: e.tensor_copy(
                out=HQF[pp][:], in_=PT[:]))(),
                reads=[rPT, rHf], writes=[rHf])
            for ns in range(2):
                fns = []
                for i in range(8):
                    kt = q4 * 8 + i
                    fns.append((lambda i=i, kt=kt, ns=ns, pp=pp, q4=q4:
                                lambda e: e.matmul(
                                    PY[:, ns * 512:(ns + 1) * 512],
                                    HQF[pp][:, i * 128:(i + 1) * 128],
                                    W2Q[:, kt * E + ns * 512:
                                        kt * E + (ns + 1) * 512],
                                    start=(q4 == 0 and i == 0),
                                    stop=(q4 == 3 and i == 7)))())
                do("t", fns, reads=[rHf, rb["W2Qw"], rPY], writes=[rPY])
        one("v", (lambda yout=yout, xres=xres:
                  lambda e: e.scalar_tensor_tensor(
                      out=yout[:], in0=PY[:], scalar=cl("a2"), in1=xres[:],
                      op0=OP.mult, op1=OP.add))(),
            reads=[rPY, rc("a2"), rxres, ryout], writes=[ryout])
        one("v", (lambda yout=yout: lambda e: e.tensor_add(
            yout[:], yout[:], PYW[:]))(),
            reads=[rPYW, ryout], writes=[ryout])
        one("y", d_load(dy[c * C:(c + 1) * C, :], yout[:]),
            reads=[ryout], writes=[rb["yst"]], inc=16)

    if _os.environ.get("K_PHA"):
        for c in range(NCH):
            one("y", d_load(dy[c * C:(c + 1) * C, :], XRES[0][:]),
                reads=[rb["XRES0"]], writes=[rb["yst"]], inc=16)
    else:
        for c in range(NCH):
            phaseB_chunk(c)

    ystk = rb["yst"].key
    final_y = cnt.get(ystk, 0)
    used = set()
    for (_, _, waits, _, _) in ops:
        used.update((w[0], w[1]) for w in waits)
    if ystk is not None:
        used.add((ystk, final_y))
    used |= set((t_[0], t_[1]) for t_ in forced)
    keyincs = {}
    for (_, _, _, tok, amt) in ops:
        if tok is not None:
            keyincs.setdefault(tok[0], []).append((tok[1], amt))
    for k in keyincs:
        keyincs[k].sort()
    remap = {}
    for k, incs in keyincs.items():
        newv = 0
        for (v, amt) in incs:
            if (k, v) in used:
                newv += amt
                remap[(k, v)] = newv
            else:
                remap[(k, v)] = None
    ops2 = []
    for (ek2, fn, waits, tok, amt) in ops:
        w2 = tuple((w[0], remap[(w[0], w[1])]) for w in waits)
        t2 = None
        if tok is not None and remap.get((tok[0], tok[1])) is not None:
            t2 = (tok[0], remap[(tok[0], tok[1])])
        ops2.append((ek2, fn, w2, t2, amt))
    final_y2 = remap.get((ystk, final_y)) if ystk is not None else None

    sems = {k: es.enter_context(nc.semaphore("sem_" + k))
            for k in ["t", "v", "s", "g", "y"] + dma_keys}

    def engine_body(ek):
        def body(e):
            wm = {}
            for (g2, fn, waits, tok, amt) in ops2:
                if g2 != ek:
                    continue
                for (we, wv) in waits:
                    if wm.get(we, 0) >= wv:
                        continue
                    e.wait_ge(sems[we], wv)
                    wm[we] = wv
                inst = fn(e)
                if tok is not None:
                    inst.then_inc(sems[tok[0]], amt)
            if ek == "y" and final_y2:
                e.wait_ge(sems[ystk], final_y2)
        return body

    engmap = {"t": "tensor", "v": "vector", "s": "scalar", "g": "gpsimd",
              "y": "sync"}
    with es, esB, nc.Block() as block:
        for ek in ["t", "v", "s", "g", "y"]:
            getattr(block, engmap[ek])(engine_body(ek))
    return nc


_NC_CACHE = None


def kernel(L0_attn_x, L0_attn_Wq, L0_attn_Wk, L0_attn_Wv, L0_attn_Wa,
           L0_attn_Wg, L0_attn_Wo, L0_attn_A_log, L0_attn_dt_bias,
           L0_attn_rms_w, L0_ffn_w1, L0_ffn_b1, L0_ffn_w2, L0_ffn_b2):
    global _NC_CACHE
    x = np.asarray(L0_attn_x, dtype=np.float32)
    B = x.shape[0]
    f32 = lambda a: np.ascontiguousarray(np.asarray(a, dtype=np.float32))
    bf16c = lambda a: np.ascontiguousarray(
        np.asarray(a, dtype=np.float32).astype(ml_dtypes.bfloat16))

    jj, ii = np.meshgrid(np.arange(128), np.arange(128), indexing="ij")
    tri = (jj <= ii).astype(np.float32)
    maskneg = np.where(jj <= ii, 0.0, NEG).astype(np.float32)
    blkh = np.zeros((128, 128), np.float32)
    for f in range(8):
        blkh[0:64, 16 * f + 2 * f] = 1.0
        blkh[64:128, 16 * f + 2 * f + 1] = 1.0
    blkht = np.zeros((16, 1024), np.float32)
    for f in range(8):
        blkht[2 * f, 128 * f:128 * f + 64] = 1.0
        blkht[2 * f + 1, 128 * f + 64:128 * (f + 1)] = 1.0
    onespar = np.zeros((128, 2), np.float32)
    onespar[0:64, 0] = 1.0
    onespar[64:128, 1] = 1.0

    base = {
        "Wq": f32(L0_attn_Wq), "Wk": f32(L0_attn_Wk), "Wv": f32(L0_attn_Wv),
        "Wg": f32(L0_attn_Wg), "Wo": f32(L0_attn_Wo), "Wa": f32(L0_attn_Wa),
        "A_log": f32(L0_attn_A_log).reshape(1, H),
        "dt_bias": f32(L0_attn_dt_bias).reshape(1, H),
        "rms_w": f32(L0_attn_rms_w).reshape(D, 1),
        "w1t": f32(np.asarray(L0_ffn_w1).T),
        "w2t": f32(np.asarray(L0_ffn_w2).T),
        "b1": f32(L0_ffn_b1).reshape(1, FF),
        "b2": f32(L0_ffn_b2).reshape(1, E),
        "tri": tri, "maskneg": bf16c(maskneg), "blkh": bf16c(blkh),
        "blkht": bf16c(blkht),
        "identb": np.eye(128, dtype=ml_dtypes.bfloat16),
        "identf": np.eye(128, dtype=np.float32),
        "onespar": bf16c(onespar),
        "ones1": np.ones((1, 128), np.float32),
        "onescol": np.ones((128, 1), np.float32),
    }
    in_maps = []
    for i in range(B):
        m = dict(base)
        m["x"] = np.ascontiguousarray(x[i])
        in_maps.append(m)

    if _NC_CACHE is None:
        _NC_CACHE = build_program()
    res = run_bass_kernel_spmd(_NC_CACHE, in_maps, core_ids=list(range(B)))
    return np.stack([np.asarray(r["y"], dtype=np.float32)
                     for r in res.results])
